# version 02
# baseline (speedup 1.0000x reference)
import contextlib
import numpy as np
import ml_dtypes
import concourse.bass as bass
import concourse.mybir as mybir
from concourse.bass_utils import run_bass_kernel_spmd
from concourse.alu_op_type import AluOpType

F32 = mybir.dt.float32
BF16 = mybir.dt.bfloat16
AF = mybir.ActivationFunctionType
BF = ml_dtypes.bfloat16
X = mybir.AxisListType.X

B_, S, D = 4, 4096, 2048
H, HD, ROPE, NOPE, LAT = 16, 128, 64, 64, 512
FFH = 8192
NQ = 2048
NSLOT = 16
INV_SQRT_HD = float(np.float32(1.0) / np.float32(np.sqrt(np.float32(128.0))))
SQRT_HD = float(np.sqrt(128.0))
TERN_B = float(0.5 + 2.0 ** -10)
F32EPS = float(np.finfo(np.float32).eps)

ENGINES = ("tensor", "vector", "scalar", "gpsimd", "sync")
P2_KV_STRIPS = 8
P2_Q_STRIPS = 4
P2_STAGE = "all"
TRUNC = 0


class Op:
    __slots__ = ("engine", "fn", "reads", "writes", "is_dma", "is_barrier",
                 "waits", "inc", "ts", "tag", "sem")

    def __init__(self, engine, fn, reads, writes, is_dma, tag=""):
        self.engine = engine
        self.fn = fn
        self.reads = tuple(reads)
        self.writes = tuple(writes)
        self.is_dma = is_dma
        self.is_barrier = False
        self.waits = []
        self.inc = None
        self.ts = 0
        self.sem = None
        self.tag = tag


class Prog:
    def __init__(self):
        self.ops = []

    def add(self, engine, fn, reads=(), writes=(), dma=False, tag=""):
        op = Op(engine, fn, reads, writes, dma, tag)
        self.ops.append(op)
        return op

    def barrier(self):
        op = Op("BARRIER", None, (), (), False, "barrier")
        op.is_barrier = True
        self.ops.append(op)

    def resolve(self, dma_pool=24):
        deps = {}
        last_writer = {}
        readers = {}
        barrier_frontier = []
        expanded = []

        def skip_edge(a, b, r):
            return (a.is_dma and b.is_dma and (r == "in" or r.endswith("_w")))

        for op in self.ops:
            if op.is_barrier:
                d = set(last_writer.values())
                for rl in readers.values():
                    d.update(rl)
                d.update(barrier_frontier)
                subs = []
                for e in ENGINES:
                    sub = Op(e, None, (), (), False, "barrier")
                    sub.is_barrier = True
                    deps[id(sub)] = set(d)
                    subs.append(sub)
                    expanded.append(sub)
                barrier_frontier = subs
                last_writer = {}
                readers = {}
                continue
            d = set(barrier_frontier)
            for r in op.reads:
                w = last_writer.get(r)
                if w is not None and not skip_edge(w, op, r):
                    d.add(w)
                readers.setdefault(r, []).append(op)
            for r in op.writes:
                w = last_writer.get(r)
                if w is not None and not skip_edge(w, op, r):
                    d.add(w)
                for rd in readers.get(r, ()):
                    if rd is not op and not skip_edge(rd, op, r):
                        d.add(rd)
                last_writer[r] = op
                readers[r] = []
            deps[id(op)] = d
            expanded.append(op)
        self.ops = expanded

        UNSAFE_SELF = {"vector", "scalar", "gpsimd"}
        needed = set()
        for op in self.ops:
            for dep in deps[id(op)]:
                if dep.engine != op.engine or dep.is_dma or dep.engine in UNSAFE_SELF:
                    needed.add(id(dep))

        eng_count = {}
        dma_idx = {}
        for op in self.ops:
            if op.is_dma:
                i = dma_idx.get(op.engine, 0)
                dma_idx[op.engine] = i + 1
                slot = i % dma_pool
                use = i // dma_pool + 1
                op.sem = (op.engine, slot)
                op.inc = 16
                op.ts = 16 * use
                if use > 1:
                    op.waits.append((op.sem, 16 * (use - 1)))
            elif id(op) in needed or op.is_barrier:
                c = eng_count.get(op.engine, 0) + 1
                eng_count[op.engine] = c
                op.sem = (op.engine, -1)
                op.inc = 1
                op.ts = c
            else:
                op.sem = None
                op.inc = None
                op.ts = 0
        frontier = {}
        for op in self.ops:
            fr = frontier.setdefault(op.engine, {})
            want = {}
            for dep in deps[id(op)]:
                if (dep.engine == op.engine and not dep.is_dma
                        and dep.engine not in UNSAFE_SELF):
                    continue
                if dep.sem is None:
                    continue
                if dep.ts > want.get(dep.sem, 0):
                    want[dep.sem] = dep.ts
            for sem, ts in sorted(want.items()):
                if fr.get(sem, 0) >= ts:
                    continue
                op.waits.append((sem, ts))
                fr[sem] = ts
        self.dma_idx = dma_idx
        return eng_count


def host_tables():
    inv = (np.float32(10000.0) ** (-(np.arange(0, ROPE, 2).astype(np.float32)) / np.float32(ROPE))).astype(np.float32)
    t = np.arange(S, dtype=np.float32)
    fr = (t[:, None] * inv[None, :]).astype(np.float32)
    return np.cos(fr).astype(np.float32), np.sin(fr).astype(np.float32)


def host_mask(p):
    r = np.arange(128)[:, None]
    c = np.arange(256)[None, :]
    return np.where(c <= p * 128 + r, np.float32(0.0), np.float32(-1e30)).astype(BF)


def strips_for_slot(j):
    blocks = 2 * j + 2
    tail = 2 if blocks % 4 == 2 else 4
    full = (blocks - tail) // 4
    return [(i * 512, 512) for i in range(full)] + [(full * 512, tail * 128)]


class Arena:

    def __init__(self, handle, n_elems):
        self.h = handle
        self.n = n_elems
        self.off = 0

    def base(self, off):
        self.off = off
        return self

    def take(self, shape, dt):
        n = shape[1] * (2 if dt == F32 else 1)
        off = (self.off + 63) & ~63
        self.off = off + n
        assert self.off <= self.n, f"arena overflow at {self.off} > {self.n}"
        sl = self.h[0:shape[0], off:off + n]
        if dt != BF16:
            sl = sl.bitcast(dt)
        return sl


def build(debug_outputs=(), phases=('P0','P1b','P1','P2','P3','P4','P5')):
    nc = bass.Bass("TRN2", target_bir_lowering=False)
    P = Prog()
    stack = contextlib.ExitStack()
    debug_outputs = set(debug_outputs)

    def inp(name, shape, dt):
        return nc.dram_tensor(name, list(shape), dt, kind="ExternalInput").ap()

    def outp(name, shape, dt):
        return nc.dram_tensor(name, list(shape), dt, kind="ExternalOutput").ap()

    def scratch(name, shape, dt):
        if name in debug_outputs:
            return nc.dram_tensor(name, list(shape), dt, kind="ExternalOutput").ap()
        return nc.dram_tensor(name, list(shape), dt).ap()

    def sbuf(name, shape, dt):
        return stack.enter_context(nc.sbuf_tensor(name, list(shape), dt))

    dma_ = lambda q, o, i, r, w, tag="": P.add(
        q, (lambda e, o=o, i=i: e.dma_start(out=o, in_=i)), r, w, dma=True, tag=tag)
    op = lambda eng, fn, r, w, tag="": P.add(eng, fn, r, w, dma=False, tag=tag)

    x_kv = inp("x_kv", (S, D), F32)
    x_own = inp("x_own", (NQ, D), F32)
    wq = inp("w_q_krope", (D, 3072), F32)
    wd = inp("w_kv_down", (D, LAT), F32)
    wu = inp("w_kv_up", (LAT, 3072), F32)
    wp = inp("w_proj", (D, D), F32)
    qg = inp("q_gain", (1, H), F32)
    wgu = inp("w_gate_up", (2 * FFH, D), F32)
    wpr = inp("w_ffn_proj", (D, FFH), F32)
    cosk = inp("cosk", (S, 32), F32)
    sink = inp("sink", (S, 32), F32)
    cosq = inp("cosq", (NQ, 32), F32)
    sinq = inp("sinq", (NQ, 32), F32)
    maskt = inp("maskt", (128, 256), BF16)
    idn16 = inp("idn16", (128, 128), BF16)
    idn32 = inp("idn32", (128, 128), F32)
    ones1 = inp("ones1", (1, 128), F32)
    epsil = inp("epsil", (128, 4), F32)
    out = outp("out_rows", (NQ, D), F32)

    wq_hi = scratch("wq_hi", (D, 3072), BF16)
    wq_lo = scratch("wq_lo", (D, 3072), BF16)
    wd_hi = scratch("wd_hi", (D, LAT), BF16)
    wd_lo = scratch("wd_lo", (D, LAT), BF16)
    wu_hi = scratch("wu_hi", (LAT, 3072), BF16)
    wu_lo = scratch("wu_lo", (LAT, 3072), BF16)
    wp_hi = scratch("wp_hi", (D, D), BF16)
    wp_lo = scratch("wp_lo", (D, D), BF16)
    guT = scratch("guT", (D, 2 * FFH), BF16)
    ppT = scratch("ppT", (FFH, D), BF16)
    q_fm = scratch("q_fm", (H, HD, NQ), F32)
    k_fm = scratch("k_fm", (H, HD, S), F32)
    v_tok = scratch("v_tok", (H, S, HD), F32)
    y_hi = scratch("y_hi", (H, HD, NQ), BF16)
    y_lo = scratch("y_lo", (H, HD, NQ), BF16)
    x2f = scratch("x2f", (NQ, D), F32)
    x2fm_hi = scratch("x2fm_hi", (D, NQ), BF16)
    x2fm_lo = scratch("x2fm_lo", (D, NQ), BF16)
    h_hi = scratch("h_hi", (NQ, FFH), BF16)
    h_lo = scratch("h_lo", (NQ, FFH), BF16)
    hn_hi = scratch("hn_hi", (FFH, NQ), BF16)
    hn_lo = scratch("hn_lo", (FFH, NQ), BF16)

    idn16_sb = sbuf("idn16_sb", (128, 128), BF16)
    idn32_sb = sbuf("idn32_sb", (128, 128), F32)
    mask_sb = sbuf("mask_sb", (128, 256), BF16)
    ones_sb = sbuf("ones_sb", (1, 128), F32)
    qg_sb = sbuf("qg_sb", (1, H), F32)
    qg_tmp = sbuf("qg_tmp", (1, H), F32)
    gain_b = sbuf("gain_b", (128, H), F32)
    negc_b = sbuf("negc_b", (128, H), F32)
    rstat = sbuf("rstat", (128, 16), F32)
    rstat2 = sbuf("rstat2", (128, 16), F32)
    rstd_ffn = sbuf("rstd_ffn", (128, NSLOT), F32)
    hsq_acc = sbuf("hsq_acc", (128, 16), F32)
    rs_sb = sbuf("rs_sb", (128, 16), F32)
    rsum = sbuf("rsum", (128, 2), F32)
    rrow = sbuf("rrow", (1, 128), F32)
    R_sb = sbuf("R_sb", (128, 128), F32)
    yf_sb = sbuf("yf_sb", (128, 128), F32)
    yhi_sb = sbuf("yhi_sb", (128, 128), BF16)
    ylo_sb = sbuf("ylo_sb", (128, 128), BF16)
    eps_sb = sbuf("eps_sb", (128, 4), F32)
    qsc = sbuf("qsc", (128, 32), BF16)
    qrs = sbuf("qrs", (128, 32), F32)
    costl = sbuf("costl", (128, 32), F32)
    sintl = sbuf("sintl", (128, 32), F32)
    ropet = [sbuf(f"ropet{i}", (128, 32), F32) for i in range(4)]

    AR_N = 88064
    AR = sbuf("AR", (128, AR_N), BF16)
    ar = Arena(AR[:], AR_N)

    ar.base(0)
    wf = [ar.take((128, 3072), F32) for _ in range(2)]
    w16p = [ar.take((128, 3072), BF16) for _ in range(2)]
    wlop = [ar.take((128, 3072), BF16) for _ in range(2)]

    ar.base(0)
    qw16 = [ar.take((128, 2048), BF16) for _ in range(2)]
    qm32 = ar.take((128, 2048), F32)
    qaux = ar.take((128, 2048), BF16)
    qq_t = ar.take((128, 2048), BF16)
    stg_gu = [ar.take((128, 512), BF16) for _ in range(16)]
    stg_pp = [ar.take((128, 2048), BF16) for _ in range(4)]
    QUANT_END = ar.off

    ar.base(0)
    xtm = [ar.take((128, D), F32) for _ in range(2)]
    xh_t = [ar.take((128, D), BF16) for _ in range(4)]
    xl_t = [ar.take((128, D), BF16) for _ in range(4)]
    xfm_hi = [ar.take((128, 512), BF16) for _ in range(16)]
    xfm_lo = [ar.take((128, 512), BF16) for _ in range(16)]
    wmov = [ar.take((128, 512), BF16) for _ in range(4)]
    praw = [ar.take((128, 512), F32) for _ in range(2)]
    psq = ar.take((128, 512), F32)
    pnrm = ar.take((128, 512), F32)
    ckv_hi = [ar.take((128, 512), BF16) for _ in range(4)]
    ckv_lo = [ar.take((128, 512), BF16) for _ in range(4)]
    ckv_f = ar.take((128, 512), F32)
    ckv_h2 = ar.take((128, 512), BF16)
    ckv_l2 = ar.take((128, 512), BF16)
    kfm_sb = [ar.take((128, 512), F32) for _ in range(H)]
    P2_END = ar.off

    ar.base(30720)
    q_sb = ar.take((128, NQ), F32)
    k_sb = [ar.take((128, 512), F32) for _ in range(2)]
    v_sb = [ar.take((128, 512), F32) for _ in range(2)]
    probs_sb = [ar.take((128, 512), F32) for _ in range(2)]
    pT_sb = [ar.take((128, 512), F32) for _ in range(2)]
    P3_END = ar.off

    ar.base(0)
    ytb = ar.take((128, 16384), BF16)
    o1_sb = [ar.take((128, D), F32) for _ in range(4)]
    xres = [ar.take((128, D), F32) for _ in range(2)]
    s1_sb = ar.take((128, D), F32)
    x2f_sb = ar.take((128, D), F32)
    x2hi_sb = ar.take((128, D), BF16)
    x2lo_sb = ar.take((128, D), BF16)
    x2stg = [ar.take((128, D), BF16) for _ in range(2)]
    psq4 = ar.take((128, 512), F32)
    wmv4 = [ar.take((128, 512), BF16) for _ in range(4)]
    P4_END = ar.off

    ar.base(0)
    ytb5 = ar.take((128, 16384), BF16)
    gmov = [ar.take((128, 512), BF16) for _ in range(3)]
    sg_sb = [ar.take((128, 512), F32) for _ in range(4)]
    hf_sb = ar.take((128, 512), F32)
    hhi_sb = ar.take((128, 512), BF16)
    hlo_sb = ar.take((128, 512), BF16)
    hsqd = ar.take((128, 512), F32)
    P5A_END = ar.off

    ar.base(0)
    hl_in = [ar.take((128, 4096), BF16) for _ in range(2)]
    hnf_sb = ar.take((128, 4096), F32)
    hn16_sb = [ar.take((128, 4096), BF16) for _ in range(2)]
    hnstg = [ar.take((128, 4096), BF16) for _ in range(2)]
    P5BP_END = ar.off

    ar.base(36864)
    hnq = [ar.take((128, 512), BF16) for _ in range(32)]
    pst = [ar.take((128, 512), BF16) for _ in range(4)]
    o2acc = [ar.take((128, 512), F32) for _ in range(16)]
    outt = ar.take((128, D), F32)
    xres5 = ar.take((128, D), F32)
    psq5 = ar.take((128, 512), F32)
    P5BJ_END = ar.off
    assert max(P2_END, P3_END, P4_END, P5A_END, P5BP_END, P5BJ_END) <= AR_N

    bk = [stack.enter_context(nc.psum_tensor(f"bk{i}", [128, 512], F32)) for i in range(6)]
    bkTT = [stack.enter_context(nc.psum_tensor(f"bkTT{i}", [128, 1024], BF16)) for i in range(2)]

    def bkT_ap(half, c0, c1):
        return bkTT[half][:, c0:c0 + (c1 - c0)]

    OUT_RES = ["out_w"]

    def emit_P0():
        dma_("sync", idn16_sb[:], idn16[:, :], ["in"], ["idn16_sb"])
        dma_("sync", idn32_sb[:], idn32[:, :], ["in"], ["idn32_sb"])
        dma_("sync", mask_sb[:], maskt[:, :], ["in"], ["mask_sb"])
        dma_("sync", ones_sb[:], ones1[:, :], ["in"], ["ones_sb"])
        dma_("sync", qg_sb[:], qg[:, :], ["in"], ["qg_sb"])
        dma_("sync", eps_sb[:], epsil[:, :], ["in"], ["eps_sb"])
        op("tensor", lambda e: e.matmul(bk[0][:, 0:H], ones_sb[:], qg_sb[:], start=True, stop=True),
           ["ones_sb", "qg_sb"], ["bk0"])
        op("vector", lambda e: e.tensor_copy(gain_b[:], bk[0][:, 0:H]), ["bk0"], ["gain_b"])
        op("scalar", lambda e: e.activation(qg_tmp[:], qg_sb[:], AF.Abs), ["qg_sb"], ["qg_tmp"])
        op("scalar", lambda e: e.mul(qg_tmp[:], qg_tmp[:], -SQRT_HD), ["qg_tmp"], ["qg_tmp"])
        op("tensor", lambda e: e.matmul(bk[0][:, 0:H], ones_sb[:], qg_tmp[:], start=True, stop=True),
           ["ones_sb", "qg_tmp"], ["bk0"])
        op("vector", lambda e: e.tensor_copy(negc_b[:], bk[0][:, 0:H]), ["bk0"], ["negc_b"])

    def emit_P1b():
        jobs = [(wq, wq_hi, wq_lo, D, 3072, "wq"), (wd, wd_hi, wd_lo, D, LAT, "wd"),
                (wu, wu_hi, wu_lo, LAT, 3072, "wu"), (wp, wp_hi, wp_lo, D, D, "wpA")]
        for (w_in, o_hi, o_lo, RR, CC, nm) in jobs:
            for r in range(RR // 128):
                i = r % 2
                rows = slice(r * 128, (r + 1) * 128)
                dma_("sync", wf[i][:, 0:CC], w_in[rows, :], ["in"], [f"wf{i}"])
                dma_("gpsimd", w16p[i][:, 0:CC], w_in[rows, :], ["in"], [f"w16p{i}"])
                op("vector",
                   lambda e, i=i, CC=CC: e.tensor_tensor(wlop[i][:, 0:CC], wf[i][:, 0:CC],
                                                         w16p[i][:, 0:CC], op=AluOpType.subtract),
                   [f"wf{i}", f"w16p{i}"], [f"wlop{i}"])
                dma_("gpsimd", o_hi[rows, :], w16p[i][:, 0:CC], [f"w16p{i}"], [nm + "_hi"])
                dma_("gpsimd", o_lo[rows, :], wlop[i][:, 0:CC], [f"wlop{i}"], [nm + "_lo"])

    def _quant_tile(i, CC):
        G = CC // 64
        op("vector",
           lambda e, i=i, CC=CC, G=G: e.tensor_reduce(
               qrs[:, 0:G], qw16[i][:, 0:CC].rearrange("p (g k) -> p g k", k=64),
               axis=X, op=AluOpType.add, apply_absolute_value=True),
           [f"qw16{i}"], ["qrs"])
        op("vector",
           lambda e, G=G: e.tensor_scalar(qsc[:, 0:G], qrs[:, 0:G], 1.0 / 64.0, 1e-8,
                                          op0=AluOpType.mult, op1=AluOpType.max),
           ["qrs"], ["qsc"])
        op("vector", lambda e, G=G: e.reciprocal(qrs[:, 0:G], qsc[:, 0:G]), ["qsc"], ["qrs"])
        op("vector",
           lambda e, i=i, CC=CC, G=G: e.tensor_tensor(
               qm32[:, 0:CC].rearrange("p (g k) -> p g k", k=64),
               qw16[i][:, 0:CC].rearrange("p (g k) -> p g k", k=64),
               qrs[:, 0:G].unsqueeze(2).broadcast_to([128, G, 64]),
               op=AluOpType.mult),
           [f"qw16{i}", "qrs"], ["qm32"])
        op("vector",
           lambda e, CC=CC: e.tensor_scalar(qaux[:, 0:CC], qm32[:, 0:CC], -TERN_B, None,
                                            op0=AluOpType.is_lt),
           ["qm32"], ["qaux"])
        op("vector",
           lambda e, CC=CC: e.scalar_tensor_tensor(qq_t[:, 0:CC], qm32[:, 0:CC], TERN_B,
                                                   qaux[:, 0:CC], op0=AluOpType.is_gt,
                                                   op1=AluOpType.subtract),
           ["qm32", "qaux"], ["qq_t"])
        op("vector",
           lambda e, CC=CC, G=G: e.tensor_tensor(
               qq_t[:, 0:CC].rearrange("p (g k) -> p g k", k=64),
               qq_t[:, 0:CC].rearrange("p (g k) -> p g k", k=64),
               qsc[:, 0:G].unsqueeze(2).broadcast_to([128, G, 64]),
               op=AluOpType.mult),
           ["qq_t", "qsc"], ["qq_t"])
        op("vector",
           lambda e, i=i, CC=CC: e.tensor_tensor(qaux[:, 0:CC], qq_t[:, 0:CC],
                                                 qw16[i][:, 0:CC], op=AluOpType.subtract),
           ["qq_t", f"qw16{i}"], ["qaux"])
        op("vector",
           lambda e, i=i, CC=CC: e.tensor_tensor(qw16[i][:, 0:CC], qw16[i][:, 0:CC],
                                                 qaux[:, 0:CC], op=AluOpType.add),
           [f"qw16{i}", "qaux"], [f"qw16{i}"])

    def emit_P1_gu(rb):
        i = rb % 2
        rows = slice(rb * 128, (rb + 1) * 128)
        dma_("gpsimd", qw16[i][:, 0:D], wgu[rows, :], ["in"], [f"qw16{i}"])
        _quant_tile(i, D)
        for c in range(16):
            half = (c % 8) // 4
            sc_ = (c % 4) * 128
            op("tensor",
               lambda e, i=i, c=c, half=half, sc_=sc_: e.transpose(
                   bkTT[half][:, sc_:sc_ + 128],
                   qw16[i][:, c * 128:(c + 1) * 128], idn16_sb[:]),
               [f"qw16{i}", "idn16_sb"], [f"bkT{half}"])
            op("vector",
               lambda e, c=c, rb=rb, half=half, sc_=sc_: e.tensor_copy(
                   stg_gu[c][:, (rb % 4) * 128:(rb % 4 + 1) * 128],
                   bkTT[half][:, sc_:sc_ + 128]),
               [f"bkT{half}"], [f"stg_gu{c}"])
        if rb % 4 == 3:
            for c in range(16):
                dma_("gpsimd", guT[c * 128:(c + 1) * 128, (rb - 3) * 128:(rb + 1) * 128],
                     stg_gu[c][:, :], [f"stg_gu{c}"], ["guT_w"])

    def emit_P1_pp(hb):
        for db in range(16):
            i = db % 2
            rows = slice(db * 128, (db + 1) * 128)
            dma_("gpsimd", qw16[i][:, 0:512], wpr[rows, hb * 512:(hb + 1) * 512],
                 ["in"], [f"qw16{i}"])
            _quant_tile(i, 512)
            for c in range(4):
                half = db % 2
                sc_ = c * 128
                op("tensor",
                   lambda e, i=i, c=c, half=half, sc_=sc_: e.transpose(
                       bkTT[half][:, sc_:sc_ + 128],
                       qw16[i][:, c * 128:(c + 1) * 128], idn16_sb[:]),
                   [f"qw16{i}", "idn16_sb"], [f"bkT{half}"])
                op("vector",
                   lambda e, c=c, db=db, half=half, sc_=sc_: e.tensor_copy(
                       stg_pp[c][:, db * 128:(db + 1) * 128],
                       bkTT[half][:, sc_:sc_ + 128]),
                   [f"bkT{half}"], [f"stg_pp{c}"])
        for c in range(4):
            dma_("gpsimd", ppT[hb * 512 + c * 128: hb * 512 + (c + 1) * 128, :],
                 stg_pp[c][:, :], [f"stg_pp{c}"], ["ppT_w"])

    def load_xfm(src_ap, row0):
        for t in range(4):
            i = t % 2
            rows = slice(row0 + t * 128, row0 + (t + 1) * 128)
            dma_("sync", xtm[i][:], src_ap[rows, :], ["in"], [f"xtm{i}"])
            op("vector", lambda e, i=i, t=t: e.tensor_copy(xh_t[t][:], xtm[i][:]),
               [f"xtm{i}"], [f"xh{t}"])
            op("vector",
               lambda e, i=i, t=t: e.tensor_tensor(xl_t[t][:], xtm[i][:], xh_t[t][:],
                                                   op=AluOpType.subtract),
               [f"xtm{i}", f"xh{t}"], [f"xl{t}"])
        for limb in range(2):
            src_t = xh_t if limb == 0 else xl_t
            pre = "xh" if limb == 0 else "xl"
            dst = xfm_hi if limb == 0 else xfm_lo
            dnm = "xfmh" if limb == 0 else "xfml"
            for dc in range(16):
                half = dc % 2
                for t in range(4):
                    op("tensor",
                       lambda e, src_t=src_t, t=t, dc=dc, half=half: e.transpose(
                           bkTT[half][:, t * 128:(t + 1) * 128],
                           src_t[t][:, dc * 128:(dc + 1) * 128], idn16_sb[:]),
                       [pre + str(t), "idn16_sb"], [f"bkT{half}"])
                op("vector",
                   lambda e, dst=dst, dc=dc, half=half: e.tensor_copy(
                       dst[dc][:, :], bkTT[half][:, 0:512]),
                   [f"bkT{half}"], [f"{dnm}{dc}"])

    def proj_group(whi_ap, wlo_ap, wc0, hires, lores):
        for dc in range(16):
            wi = (dc % 2) * 2
            dma_("sync", wmov[wi][:, :], whi_ap[dc * 128:(dc + 1) * 128, wc0:wc0 + 512],
                 [hires], [f"wmov{wi}"])
            dma_("sync", wmov[wi + 1][:, :], wlo_ap[dc * 128:(dc + 1) * 128, wc0:wc0 + 512],
                 [lores], [f"wmov{wi + 1}"])
            for t in range(4):
                plan = [(xfm_hi, "xfmh", wi), (xfm_lo, "xfml", wi), (xfm_hi, "xfmh", wi + 1)]
                for pi, (xt, xn, wj) in enumerate(plan):
                    st = (dc == 0 and pi == 0)
                    sp = (dc == 15 and pi == 2)
                    op("tensor",
                       lambda e, xt=xt, dc=dc, t=t, wj=wj, st=st, sp=sp: e.matmul(
                           bk[t][:, :], xt[dc][:, t * 128:(t + 1) * 128],
                           wmov[wj][:, :], start=st, stop=sp),
                       [f"{xn}{dc}", f"wmov{wj}"], [f"bk{t}"])

    def kvup_group(g):
        for l in range(4):
            wi = (l % 2) * 2
            dma_("sync", wmov[wi][:, :], wu_hi[l * 128:(l + 1) * 128, g * 512:(g + 1) * 512],
                 ["wu_hi"], [f"wmov{wi}"])
            dma_("sync", wmov[wi + 1][:, :], wu_lo[l * 128:(l + 1) * 128, g * 512:(g + 1) * 512],
                 ["wu_lo"], [f"wmov{wi + 1}"])
            for t in range(4):
                plan = [(ckv_hi, "ckvh", wi), (ckv_lo, "ckvl", wi), (ckv_hi, "ckvh", wi + 1)]
                for pi, (ct, cn, wj) in enumerate(plan):
                    st = (l == 0 and pi == 0)
                    sp = (l == 3 and pi == 2)
                    op("tensor",
                       lambda e, ct=ct, l=l, t=t, wj=wj, st=st, sp=sp: e.matmul(
                           bk[t][:, :], ct[l][:, t * 128:(t + 1) * 128],
                           wmov[wj][:, :], start=st, stop=sp),
                       [f"{cn}{l}", f"wmov{wj}"], [f"bk{t}"])

    def rms_units(t, bank_name, bank_ap, with_rope, cos_ap, sin_ap, row0,
                  head0, gain, units_per_head, dst_rows_fixed=None):
        pr, prn = praw[t % 2], f"praw{t % 2}"
        op("scalar", lambda e: e.copy(pr[:], bank_ap[:, :]), [bank_name], [prn])
        op("scalar", lambda e: e.activation(psq[:], pr[:], AF.Square), [prn], ["psq"])
        op("vector",
           lambda e: e.tensor_reduce(rstat[:, 0:8], psq[:].rearrange("p (g k) -> p g k", k=64),
                                     axis=X, op=AluOpType.add),
           ["psq"], ["rstat"])
        op("scalar",
           lambda e: e.activation(rstat2[:, 0:8], rstat[:, 0:8], AF.Sqrt,
                                  bias=eps_sb[:, 0:1], scale=1.0 / 64.0),
           ["rstat", "eps_sb"], ["rstat2"])
        op("vector", lambda e: e.reciprocal(rstat[:, 0:8], rstat2[:, 0:8]), ["rstat2"], ["rstat"])
        if gain:
            for c in range(4):
                hh = head0 + c
                op("vector",
                   lambda e, c=c, hh=hh: e.tensor_scalar(rstat[:, 2 * c:2 * c + 2],
                                                         rstat[:, 2 * c:2 * c + 2],
                                                         gain_b[:, hh:hh + 1], None,
                                                         op0=AluOpType.mult),
                   ["rstat", "gain_b"], ["rstat"])
        if with_rope:
            dma_("sync", costl[:], cos_ap[row0:row0 + 128, :], ["in"], ["costl"])
            dma_("sync", sintl[:], sin_ap[row0:row0 + 128, :], ["in"], ["sintl"])
        for u in range(8):
            u0 = u * 64
            rope_u = with_rope and (units_per_head == 1 or (u % 2 == 1))
            if rope_u:
                for (di, a_sl, ct, cn) in [(0, slice(u0, u0 + 32), costl, "costl"),
                                           (1, slice(u0 + 32, u0 + 64), sintl, "sintl"),
                                           (2, slice(u0, u0 + 32), sintl, "sintl"),
                                           (3, slice(u0 + 32, u0 + 64), costl, "costl")]:
                    op("vector",
                       lambda e, di=di, a_sl=a_sl, ct=ct, pr=pr: e.tensor_tensor(
                           ropet[di][:], pr[:, a_sl], ct[:], op=AluOpType.mult),
                       [prn, cn], [f"ropet{di}"])
                op("vector",
                   lambda e, u0=u0: e.tensor_tensor(pnrm[:, u0:u0 + 32], ropet[0][:],
                                                    ropet[1][:], op=AluOpType.add),
                   ["ropet0", "ropet1"], ["pnrm"])
                op("vector",
                   lambda e, u0=u0: e.tensor_tensor(pnrm[:, u0 + 32:u0 + 64], ropet[3][:],
                                                    ropet[2][:], op=AluOpType.subtract),
                   ["ropet3", "ropet2"], ["pnrm"])
                src, srcn = pnrm, "pnrm"
            else:
                src, srcn = pr, prn
            op("vector",
               lambda e, u=u, u0=u0, src=src: e.tensor_scalar(
                   pnrm[:, u0:u0 + 64], src[:, u0:u0 + 64], rstat[:, u:u + 1], None,
                   op0=AluOpType.mult),
               [srcn, "rstat"], ["pnrm"])
        for up in range(4):
            u0 = up * 128
            op("tensor",
               lambda e, u0=u0: e.transpose(bk[4][:, 0:128], pnrm[:, u0:u0 + 128],
                                            idn32_sb[:]),
               ["pnrm", "idn32_sb"], ["bk4"])
            if units_per_head == 2:
                head = head0 + up
                op("vector",
                   lambda e, head=head, t=t: e.tensor_copy(
                       kfm_sb[head][:, t * 128:(t + 1) * 128], bk[4][:, 0:128]),
                   ["bk4"], [f"kfmsb{head}"])
            else:
                for half in range(2):
                    head = head0 + 2 * up + half
                    op("vector",
                       lambda e, head=head, half=half, t=t: e.tensor_copy(
                           kfm_sb[head][dst_rows_fixed:dst_rows_fixed + 64,
                                        t * 128:(t + 1) * 128],
                           bk[4][half * 64:(half + 1) * 64, 0:128]),
                       ["bk4"], [f"kfmsb{head}"])

    def emit_P2_kv_strip(s):
        T0 = s * 512
        load_xfm(x_kv, T0)
        if P2_STAGE == "xfm":
            return
        for g, (whi_ap, wlo_ap, wc0, hr, lr) in enumerate([
                (wq_hi, wq_lo, 2048, "wq_hi", "wq_lo"),
                (wq_hi, wq_lo, 2560, "wq_hi", "wq_lo"),
                (wd_hi, wd_lo, 0, "wd_hi", "wd_lo")]):
            proj_group(whi_ap, wlo_ap, wc0, hr, lr)
            if P2_STAGE == "proj":
                continue
            for t in range(4):
                if g < 2:
                    rms_units(t, f"bk{t}", bk[t], True, cosk, sink, T0 + t * 128,
                              g * 8, False, 1, dst_rows_fixed=64)
                else:
                    pr, prn = praw[t % 2], f"praw{t % 2}"
                    op("scalar", lambda e, t=t, pr=pr: e.copy(pr[:], bk[t][:, :]),
                       [f"bk{t}"], [prn])
                    op("scalar", lambda e, pr=pr: e.activation(psq[:], pr[:], AF.Square),
                       [prn], ["psq"])
                    op("vector",
                       lambda e: e.tensor_reduce(rstat[:, 0:1], psq[:], axis=X,
                                                 op=AluOpType.add),
                       ["psq"], ["rstat"])
                    op("scalar",
                       lambda e: e.activation(rstat2[:, 0:1], rstat[:, 0:1], AF.Sqrt,
                                              bias=eps_sb[:, 0:1], scale=1.0 / 512.0),
                       ["rstat", "eps_sb"], ["rstat2"])
                    op("vector", lambda e: e.reciprocal(rstat[:, 0:1], rstat2[:, 0:1]),
                       ["rstat2"], ["rstat"])
                    op("vector",
                       lambda e, pr=pr: e.tensor_scalar(ckv_f[:], pr[:], rstat[:, 0:1],
                                                        None, op0=AluOpType.mult),
                       [prn, "rstat"], ["ckvf"])
                    op("vector", lambda e: e.tensor_copy(ckv_h2[:], ckv_f[:]),
                       ["ckvf"], ["ckvh2"])
                    op("vector",
                       lambda e: e.tensor_tensor(ckv_l2[:], ckv_f[:], ckv_h2[:],
                                                 op=AluOpType.subtract),
                       ["ckvf", "ckvh2"], ["ckvl2"])
                    for limb in range(2):
                        srct = ckv_h2 if limb == 0 else ckv_l2
                        srcn = "ckvh2" if limb == 0 else "ckvl2"
                        dstt = ckv_hi if limb == 0 else ckv_lo
                        dnm = "ckvh" if limb == 0 else "ckvl"
                        for l in range(4):
                            half = l % 2
                            op("tensor",
                               lambda e, srct=srct, l=l, half=half: e.transpose(
                                   bkTT[half][:, 0:128],
                                   srct[:, l * 128:(l + 1) * 128], idn16_sb[:]),
                               [srcn, "idn16_sb"], [f"bkT{half}"])
                            op("vector",
                               lambda e, dstt=dstt, l=l, t=t, half=half: e.tensor_copy(
                                   dstt[l][:, t * 128:(t + 1) * 128],
                                   bkTT[half][:, 0:128]),
                               [f"bkT{half}"], [f"{dnm}{l}"])
        if P2_STAGE in ("xfm", "proj", "post"):
            for head in range(H):
                dma_("gpsimd", k_fm[head, :, T0:T0 + 512], kfm_sb[head][:, :],
                     [f"kfmsb{head}"], ["k_fm_w"])
            return
        for g in range(6):
            kvup_group(g)
            for t in range(4):
                if g < 2:
                    rms_units(t, f"bk{t}", bk[t], False, None, None, 0,
                              g * 8, False, 1, dst_rows_fixed=0)
                else:
                    pr, prn = praw[t % 2], f"praw{t % 2}"
                    op("scalar", lambda e, t=t, pr=pr: e.copy(pr[:], bk[t][:, :]),
                       [f"bk{t}"], [prn])
                    for c in range(4):
                        head = (g - 2) * 4 + c
                        dma_("gpsimd", v_tok[head, T0 + t * 128: T0 + (t + 1) * 128, :],
                             pr[:, c * 128:(c + 1) * 128], [prn], ["v_tok_w"])
        for head in range(H):
            dma_("gpsimd", k_fm[head, :, T0:T0 + 512], kfm_sb[head][:, :],
                 [f"kfmsb{head}"], ["k_fm_w"])

    def emit_P2_q_strip(qs):
        T0 = qs * 512
        load_xfm(x_own, T0)
        for g in range(4):
            proj_group(wq_hi, wq_lo, g * 512, "wq_hi", "wq_lo")
            for t in range(4):
                rms_units(t, f"bk{t}", bk[t], True, cosq, sinq, T0 + t * 128,
                          g * 4, True, 2)
        for head in range(H):
            dma_("gpsimd", q_fm[head, :, T0:T0 + 512], kfm_sb[head][:, :],
                 [f"kfmsb{head}"], ["q_fm_w"])

    def emit_P3_head(h):
        dma_("sync", q_sb[:], q_fm[h, :, :], ["q_fm_w"], ["q_sb"])
        for j in range(NSLOT):
            strips = strips_for_slot(j)
            nst = len(strips)
            for si, (k0, kw) in enumerate(strips):
                bi = si % 2
                dma_("sync", k_sb[bi][:, 0:kw], k_fm[h, :, k0:k0 + kw], ["k_fm_w"],
                     [f"k_sb{bi}"])
                dma_("sync", v_sb[bi][:, 0:kw].rearrange("p (c t) -> p c t", t=128),
                     v_tok[h, k0:k0 + kw, :].rearrange("(c p) t -> p c t", p=128),
                     ["v_tok_w"], [f"v_sb{bi}"])
                sc = 2 + bi
                last = (si == nst - 1)
                if last:
                    op("tensor",
                       lambda e, sc=sc, kw=kw: e.matmul(bk[sc][:, kw - 256:kw], idn16_sb[:],
                                                        mask_sb[:], start=True, stop=False),
                       ["idn16_sb", "mask_sb"], [f"bk{sc}"])
                op("tensor",
                   lambda e, sc=sc, j=j, kw=kw, bi=bi, st=(not last): e.matmul(
                       bk[sc][:, 0:kw], q_sb[:, j * 128:(j + 1) * 128],
                       k_sb[bi][:, 0:kw], start=st, stop=True),
                   ["q_sb", f"k_sb{bi}"], [f"bk{sc}"])
                op("scalar",
                   lambda e, sc=sc, kw=kw, bi=bi, h=h, si=si: e.activation(
                       probs_sb[bi][:, 0:kw], bk[sc][:, 0:kw], AF.Exp,
                       bias=negc_b[:, h:h + 1], scale=INV_SQRT_HD,
                       accum_out=rs_sb[:, si:si + 1]),
                   [f"bk{sc}", "negc_b"], [f"probs{bi}", "rs_sb"])
                for c in range(kw // 128):
                    op("tensor",
                       lambda e, bi=bi, c=c: e.transpose(
                           bk[4][:, c * 128:(c + 1) * 128],
                           probs_sb[bi][:, c * 128:(c + 1) * 128], idn32_sb[:]),
                       [f"probs{bi}", "idn32_sb"], ["bk4"])
                op("vector",
                   lambda e, bi=bi, kw=kw: e.tensor_copy(pT_sb[bi][:, 0:kw], bk[4][:, 0:kw]),
                   ["bk4"], [f"pT{bi}"])
                for c in range(kw // 128):
                    st = (si == 0 and c == 0)
                    sp = (last and c == kw // 128 - 1)
                    op("tensor",
                       lambda e, bi=bi, c=c, st=st, sp=sp: e.matmul(
                           bk[0][:, 0:128], v_sb[bi][:, c * 128:(c + 1) * 128],
                           pT_sb[bi][:, c * 128:(c + 1) * 128], start=st, stop=sp),
                       [f"v_sb{bi}", f"pT{bi}"], ["bk0"])
            op("vector",
               lambda e, nst=nst: e.tensor_reduce(rsum[:, 0:1], rs_sb[:, 0:nst], axis=X,
                                                  op=AluOpType.add),
               ["rs_sb"], ["rsum"])
            op("vector", lambda e: e.reciprocal(rsum[:, 1:2], rsum[:, 0:1]), ["rsum"], ["rsum"])
            op("tensor",
               lambda e: e.transpose(bk[5][0:1, 0:128], rsum[:, 1:2], idn32_sb[:]),
               ["rsum", "idn32_sb"], ["bk5"])
            op("vector", lambda e: e.tensor_copy(rrow[:, :], bk[5][0:1, 0:128]),
               ["bk5"], ["rrow"])
            op("tensor",
               lambda e: e.matmul(bk[5][:, 0:128], ones_sb[:], rrow[:, :],
                                  start=True, stop=True),
               ["ones_sb", "rrow"], ["bk5"])
            op("vector", lambda e: e.tensor_copy(R_sb[:], bk[5][:, 0:128]), ["bk5"], ["R_sb"])
            op("vector",
               lambda e: e.tensor_tensor(yf_sb[:], bk[0][:, 0:128], R_sb[:],
                                         op=AluOpType.mult),
               ["bk0", "R_sb"], ["yf_sb"])
            op("vector", lambda e: e.tensor_copy(yhi_sb[:], yf_sb[:]), ["yf_sb"], ["yhi_sb"])
            op("vector",
               lambda e: e.tensor_tensor(ylo_sb[:], yf_sb[:], yhi_sb[:],
                                         op=AluOpType.subtract),
               ["yf_sb", "yhi_sb"], ["ylo_sb"])
            dma_("gpsimd", y_hi[h, :, j * 128:(j + 1) * 128], yhi_sb[:], ["yhi_sb"], ["y_hi_w"])
            dma_("gpsimd", y_lo[h, :, j * 128:(j + 1) * 128], ylo_sb[:], ["ylo_sb"], ["y_lo_w"])

    def emit_P4_tb(tb):
        t0 = tb * 4
        for fc in range(H):
            for limb, (src, nm) in enumerate([(y_hi, "y_hi_w"), (y_lo, "y_lo_w")]):
                dma_("sync", ytb[:, fc * 1024 + limb * 512: fc * 1024 + (limb + 1) * 512],
                     src[fc, :, t0 * 128:(t0 + 4) * 128], [nm], ["ytb"])
        for dgrp in range(4):
            for fc in range(H):
                wi = (fc % 2) * 2
                dma_("sync", wmv4[wi][:, :],
                     wp_hi[fc * 128:(fc + 1) * 128, dgrp * 512:(dgrp + 1) * 512],
                     ["wpA_hi"], [f"wmv4{wi}"])
                dma_("sync", wmv4[wi + 1][:, :],
                     wp_lo[fc * 128:(fc + 1) * 128, dgrp * 512:(dgrp + 1) * 512],
                     ["wpA_lo"], [f"wmv4{wi + 1}"])
                for t in range(4):
                    yh0 = fc * 1024 + t * 128
                    yl0 = fc * 1024 + 512 + t * 128
                    for pi, (y0, wj) in enumerate([(yh0, wi), (yl0, wi), (yh0, wi + 1)]):
                        st = (fc == 0 and pi == 0)
                        sp = (fc == 15 and pi == 2)
                        op("tensor",
                           lambda e, y0=y0, wj=wj, t=t, st=st, sp=sp: e.matmul(
                               bk[t][:, :], ytb[:, y0:y0 + 128], wmv4[wj][:, :],
                               start=st, stop=sp),
                           ["ytb", f"wmv4{wj}"], [f"bk{t}"])
            for t in range(4):
                op("scalar",
                   lambda e, t=t, dgrp=dgrp: e.copy(o1_sb[t][:, dgrp * 512:(dgrp + 1) * 512],
                                                    bk[t][:, :]),
                   [f"bk{t}"], [f"o1sb{t}"])
        for t in range(4):
            gt = t0 + t
            xi = t % 2
            dma_("sync", xres[xi][:], x_own[gt * 128:(gt + 1) * 128, :], ["in"], [f"xres{xi}"])
            op("vector",
               lambda e, xi=xi, t=t: e.scalar_tensor_tensor(
                   s1_sb[:], xres[xi][:], 1.0, o1_sb[t][:], op0=AluOpType.mult,
                   op1=AluOpType.add, accum_out=rstat[:, 0:1]),
               [f"xres{xi}", f"o1sb{t}"], ["s1_sb", "rstat"])
            for c in range(4):
                op("scalar",
                   lambda e, c=c: e.activation(psq4[:], s1_sb[:, c * 512:(c + 1) * 512],
                                               AF.Square, accum_out=rstat2[:, c:c + 1]),
                   ["s1_sb"], ["psq4", "rstat2"])
            op("vector",
               lambda e: e.tensor_reduce(rstat2[:, 4:5], rstat2[:, 0:4], axis=X,
                                         op=AluOpType.add),
               ["rstat2"], ["rstat2"])
            op("vector",
               lambda e: e.tensor_scalar(rstat[:, 1:2], rstat[:, 0:1], 1.0 / 2048.0, None,
                                         op0=AluOpType.mult),
               ["rstat"], ["rstat"])
            op("vector",
               lambda e: e.tensor_tensor(rstat[:, 2:3], rstat[:, 1:2], rstat[:, 1:2],
                                         op=AluOpType.mult),
               ["rstat"], ["rstat"])
            op("vector",
               lambda e: e.scalar_tensor_tensor(rstat[:, 3:4], rstat2[:, 4:5], 1.0 / 2048.0,
                                                rstat[:, 2:3], op0=AluOpType.mult,
                                                op1=AluOpType.subtract),
               ["rstat2", "rstat"], ["rstat"])
            op("scalar",
               lambda e: e.activation(rstat[:, 4:5], rstat[:, 3:4], AF.Sqrt,
                                      bias=eps_sb[:, 1:2]),
               ["rstat", "eps_sb"], ["rstat"])
            op("vector", lambda e: e.reciprocal(rstat[:, 5:6], rstat[:, 4:5]),
               ["rstat"], ["rstat"])
            op("vector",
               lambda e: e.tensor_scalar(x2f_sb[:], s1_sb[:], rstat[:, 1:2], rstat[:, 5:6],
                                         op0=AluOpType.subtract, op1=AluOpType.mult),
               ["s1_sb", "rstat"], ["x2f_sb"])
            op("vector", lambda e: e.tensor_copy(x2hi_sb[:], x2f_sb[:]), ["x2f_sb"], ["x2hi_sb"])
            op("vector",
               lambda e: e.tensor_tensor(x2lo_sb[:], x2f_sb[:], x2hi_sb[:],
                                         op=AluOpType.subtract),
               ["x2f_sb", "x2hi_sb"], ["x2lo_sb"])
            dma_("gpsimd", x2f[gt * 128:(gt + 1) * 128, :], x2f_sb[:], ["x2f_sb"], ["x2f_w"])
            for limb, (srct, srcn, dst_ap, dnm) in enumerate([
                    (x2hi_sb, "x2hi_sb", x2fm_hi, "x2fm_hi_w"),
                    (x2lo_sb, "x2lo_sb", x2fm_lo, "x2fm_lo_w")]):
                for dc in range(16):
                    half = dc % 2
                    op("tensor",
                       lambda e, srct=srct, dc=dc, half=half: e.transpose(
                           bkTT[half][:, 0:128],
                           srct[:, dc * 128:(dc + 1) * 128], idn16_sb[:]),
                       [srcn, "idn16_sb"], [f"bkT{half}"])
                    op("vector",
                       lambda e, limb=limb, dc=dc, half=half: e.tensor_copy(
                           x2stg[limb][:, dc * 128:(dc + 1) * 128],
                           bkTT[half][:, 0:128]),
                       [f"bkT{half}"], [f"x2stg{limb}"])
                dma_("gpsimd",
                     dst_ap[:, gt * 128:(gt + 1) * 128].rearrange("(c p) t -> p c t", p=128),
                     x2stg[limb][:, :].rearrange("p (c t) -> p c t", t=128),
                     [f"x2stg{limb}"], [dnm])

    def emit_P5a_tb(tb):
        t0 = tb * 4
        for t in range(4):
            gt = t0 + t
            for limb, (src_ap, nm) in enumerate([(x2fm_hi, "x2fm_hi_w"), (x2fm_lo, "x2fm_lo_w")]):
                dma_("sync",
                     ytb5[:, (t * 2 + limb) * 2048:(t * 2 + limb + 1) * 2048].rearrange(
                         "p (c t) -> p c t", t=128),
                     src_ap[:, gt * 128:(gt + 1) * 128].rearrange("(c p) t -> p c t", p=128),
                     [nm], ["ytb5"])
        for hp in range(16):
            for gi, col0 in enumerate([hp * 512, FFH + hp * 512]):
                for dc in range(16):
                    mi = dc % 3
                    dma_("sync", gmov[mi][:, :], guT[dc * 128:(dc + 1) * 128, col0:col0 + 512],
                         ["guT_w"], [f"gmov{mi}"])
                    for t in range(4):
                        for limb in range(2):
                            st = (dc == 0 and limb == 0)
                            sp = (dc == 15 and limb == 1)
                            y0 = (t * 2 + limb) * 2048 + dc * 128
                            op("tensor",
                               lambda e, y0=y0, mi=mi, t=t, st=st, sp=sp: e.matmul(
                                   bk[t][:, :], ytb5[:, y0:y0 + 128], gmov[mi][:, :],
                                   start=st, stop=sp),
                               ["ytb5", f"gmov{mi}"], [f"bk{t}"])
                for t in range(4):
                    if gi == 0:
                        op("scalar",
                           lambda e, t=t: e.activation(sg_sb[t][:], bk[t][:, :], AF.Silu),
                           [f"bk{t}"], [f"sgsb{t}"])
                    else:
                        gt = t0 + t
                        op("vector",
                           lambda e, t=t: e.tensor_tensor(hf_sb[:], sg_sb[t][:], bk[t][:, :],
                                                          op=AluOpType.mult),
                           [f"sgsb{t}", f"bk{t}"], ["hf_sb"])
                        op("vector", lambda e: e.tensor_copy(hhi_sb[:], hf_sb[:]),
                           ["hf_sb"], ["hhi_sb"])
                        op("vector",
                           lambda e: e.tensor_tensor(hlo_sb[:], hf_sb[:], hhi_sb[:],
                                                     op=AluOpType.subtract),
                           ["hf_sb", "hhi_sb"], ["hlo_sb"])
                        dma_("gpsimd", h_hi[gt * 128:(gt + 1) * 128, hp * 512:(hp + 1) * 512],
                             hhi_sb[:], ["hhi_sb"], ["h_hi_w"])
                        dma_("gpsimd", h_lo[gt * 128:(gt + 1) * 128, hp * 512:(hp + 1) * 512],
                             hlo_sb[:], ["hlo_sb"], ["h_lo_w"])

    def emit_P5b_prep(gt):
        for half in range(2):
            cols = slice(half * 4096, (half + 1) * 4096)
            dma_("sync", hl_in[0][:, :], h_hi[gt * 128:(gt + 1) * 128, cols], ["h_hi_w"],
                 ["hlin0"])
            dma_("sync", hl_in[1][:, :], h_lo[gt * 128:(gt + 1) * 128, cols], ["h_lo_w"],
                 ["hlin1"])
            op("vector",
               lambda e: e.tensor_tensor(hnf_sb[:], hl_in[0][:], hl_in[1][:],
                                         op=AluOpType.add),
               ["hlin0", "hlin1"], ["hnf_sb"])
            for c in range(8):
                op("scalar",
                   lambda e, c=c, half=half: e.activation(
                       hsqd[:], hnf_sb[:, c * 512:(c + 1) * 512], AF.Square,
                       accum_out=hsq_acc[:, half * 8 + c: half * 8 + c + 1]),
                   ["hnf_sb"], ["hsqd", "hsq_acc"])
        op("vector",
           lambda e: e.tensor_reduce(rstat[:, 0:1], hsq_acc[:, 0:16], axis=X,
                                     op=AluOpType.add),
           ["hsq_acc"], ["rstat"])
        op("scalar",
           lambda e: e.activation(rstat[:, 1:2], rstat[:, 0:1], AF.Sqrt,
                                  bias=eps_sb[:, 2:3], scale=1.0 / FFH),
           ["rstat", "eps_sb"], ["rstat"])
        op("vector", lambda e, gt=gt: e.reciprocal(rstd_ffn[:, gt:gt + 1], rstat[:, 1:2]),
           ["rstat"], ["rstd_ffn"])
        for half in range(2):
            cols = slice(half * 4096, (half + 1) * 4096)
            dma_("sync", hl_in[0][:, :], h_hi[gt * 128:(gt + 1) * 128, cols], ["h_hi_w"],
                 ["hlin0"])
            dma_("sync", hl_in[1][:, :], h_lo[gt * 128:(gt + 1) * 128, cols], ["h_lo_w"],
                 ["hlin1"])
            op("vector",
               lambda e: e.tensor_tensor(hnf_sb[:], hl_in[0][:], hl_in[1][:],
                                         op=AluOpType.add),
               ["hlin0", "hlin1"], ["hnf_sb"])
            op("vector",
               lambda e, gt=gt: e.tensor_scalar(hn16_sb[0][:], hnf_sb[:],
                                                rstd_ffn[:, gt:gt + 1], None,
                                                op0=AluOpType.mult),
               ["hnf_sb", "rstd_ffn"], ["hn16_0"])
            op("vector",
               lambda e, gt=gt: e.tensor_scalar(hnf_sb[:], hnf_sb[:],
                                                rstd_ffn[:, gt:gt + 1], None,
                                                op0=AluOpType.mult),
               ["hnf_sb", "rstd_ffn"], ["hnf_sb"])
            op("vector",
               lambda e: e.tensor_tensor(hn16_sb[1][:], hnf_sb[:], hn16_sb[0][:],
                                         op=AluOpType.subtract),
               ["hnf_sb", "hn16_0"], ["hn16_1"])
            for limb, (dst_ap, dnm) in enumerate([(hn_hi, "hn_hi_w"), (hn_lo, "hn_lo_w")]):
                for hc in range(32):
                    hhalf = hc % 2
                    op("tensor",
                       lambda e, limb=limb, hc=hc, hhalf=hhalf: e.transpose(
                           bkTT[hhalf][:, 0:128],
                           hn16_sb[limb][:, hc * 128:(hc + 1) * 128], idn16_sb[:]),
                       [f"hn16_{limb}", "idn16_sb"], [f"bkT{hhalf}"])
                    op("vector",
                       lambda e, limb=limb, hc=hc, hhalf=hhalf: e.tensor_copy(
                           hnstg[limb][:, hc * 128:(hc + 1) * 128],
                           bkTT[hhalf][:, 0:128]),
                       [f"bkT{hhalf}"], [f"hnstg{limb}"])
                dma_("gpsimd",
                     dst_ap[half * 4096:(half + 1) * 4096,
                            gt * 128:(gt + 1) * 128].rearrange("(c p) t -> p c t", p=128),
                     hnstg[limb][:, :].rearrange("p (c t) -> p c t", t=128),
                     [f"hnstg{limb}"], [dnm])

    def emit_P5b_proj(sb_i):
        tok0 = sb_i * 512
        for hq in range(4):
            for hc in range(16):
                for limb, (src_ap, nm) in enumerate([(hn_hi, "hn_hi_w"), (hn_lo, "hn_lo_w")]):
                    dma_("sync", hnq[hc * 2 + limb][:, :],
                         src_ap[hq * 2048 + hc * 128: hq * 2048 + (hc + 1) * 128,
                                tok0:tok0 + 512],
                         [nm], [f"hnq{hc * 2 + limb}"])
            for dcq in range(4):
                for hc in range(16):
                    pi = hc % 4
                    dma_("sync", pst[pi][:, :],
                         ppT[hq * 2048 + hc * 128: hq * 2048 + (hc + 1) * 128,
                             dcq * 512:(dcq + 1) * 512],
                         ["ppT_w"], [f"pst{pi}"])
                    for dc in range(4):
                        for limb in range(2):
                            st = (hc == 0 and limb == 0)
                            sp = (hc == 15 and limb == 1)
                            op("tensor",
                               lambda e, pi=pi, dc=dc, limb=limb, st=st, sp=sp, hc=hc: e.matmul(
                                   bk[dc][:, :], pst[pi][:, dc * 128:(dc + 1) * 128],
                                   hnq[hc * 2 + limb][:, :], start=st, stop=sp),
                               [f"pst{pi}", f"hnq{hc * 2 + limb}"], [f"bk{dc}"])
                for dc in range(4):
                    di = dcq * 4 + dc
                    if hq == 0:
                        op("vector",
                           lambda e, dc=dc, di=di: e.tensor_copy(o2acc[di][:], bk[dc][:, :]),
                           [f"bk{dc}"], [f"o2acc{di}"])
                    else:
                        op("vector",
                           lambda e, dc=dc, di=di: e.tensor_tensor(
                               o2acc[di][:], bk[dc][:, :], o2acc[di][:], op=AluOpType.add),
                           [f"bk{dc}", f"o2acc{di}"], [f"o2acc{di}"])
        for t in range(4):
            gt = sb_i * 4 + t
            for di in range(16):
                op("tensor",
                   lambda e, di=di, t=t: e.transpose(
                       bk[4][:, 0:128], o2acc[di][:, t * 128:(t + 1) * 128], idn32_sb[:]),
                   [f"o2acc{di}", "idn32_sb"], ["bk4"])
                op("vector",
                   lambda e, di=di: e.tensor_copy(outt[:, di * 128:(di + 1) * 128],
                                                  bk[4][:, 0:128]),
                   ["bk4"], ["outt"])
            dma_("sync", xres5[:], x2f[gt * 128:(gt + 1) * 128, :], ["x2f_w"], ["xres5"])
            op("vector",
               lambda e: e.scalar_tensor_tensor(outt[:], xres5[:], 1.0, outt[:],
                                                op0=AluOpType.mult, op1=AluOpType.add,
                                                accum_out=rstat[:, 0:1]),
               ["xres5", "outt"], ["outt", "rstat"])
            for c in range(4):
                op("scalar",
                   lambda e, c=c: e.activation(psq5[:], outt[:, c * 512:(c + 1) * 512],
                                               AF.Square, accum_out=rstat2[:, c:c + 1]),
                   ["outt"], ["psq5", "rstat2"])
            op("vector",
               lambda e: e.tensor_reduce(rstat2[:, 4:5], rstat2[:, 0:4], axis=X,
                                         op=AluOpType.add),
               ["rstat2"], ["rstat2"])
            op("vector",
               lambda e: e.tensor_scalar(rstat[:, 1:2], rstat[:, 0:1], 1.0 / 2048.0, None,
                                         op0=AluOpType.mult),
               ["rstat"], ["rstat"])
            op("vector",
               lambda e: e.tensor_tensor(rstat[:, 2:3], rstat[:, 1:2], rstat[:, 1:2],
                                         op=AluOpType.mult),
               ["rstat"], ["rstat"])
            op("vector",
               lambda e: e.scalar_tensor_tensor(rstat[:, 3:4], rstat2[:, 4:5], 1.0 / 2048.0,
                                                rstat[:, 2:3], op0=AluOpType.mult,
                                                op1=AluOpType.subtract),
               ["rstat2", "rstat"], ["rstat"])
            op("scalar",
               lambda e: e.activation(rstat[:, 4:5], rstat[:, 3:4], AF.Sqrt,
                                      bias=eps_sb[:, 1:2]),
               ["rstat", "eps_sb"], ["rstat"])
            op("vector", lambda e: e.reciprocal(rstat[:, 5:6], rstat[:, 4:5]),
               ["rstat"], ["rstat"])
            op("vector",
               lambda e: e.tensor_scalar(xres5[:], outt[:], rstat[:, 1:2], rstat[:, 5:6],
                                         op0=AluOpType.subtract, op1=AluOpType.mult),
               ["outt", "rstat"], ["xres5"])
            dma_("gpsimd", out[gt * 128:(gt + 1) * 128, :], xres5[:], ["xres5"], ["out_w"])

    if "P0" in phases:
        emit_P0()
    if "P1b" in phases:
        emit_P1b()
        P.barrier()
    if "P1" in phases and "P3" not in phases:
        for rb in range(2 * FFH // 128):
            emit_P1_gu(rb)
        for hb in range(FFH // 512):
            emit_P1_pp(hb)
        P.barrier()
    if "P2" in phases:
        for s in range(P2_KV_STRIPS):
            emit_P2_kv_strip(s)
        for qs in range(P2_Q_STRIPS):
            emit_P2_q_strip(qs)
        P.barrier()
    if "P3" in phases:
        quant_units = ([("gu", rb) for rb in range(2 * FFH // 128)] +
                       [("pp", hb) for hb in range(FFH // 512)]) if "P1" in phases else []
        per_head = (len(quant_units) + H - 1) // H
        for h in range(H):
            for kind, i in quant_units[h * per_head:(h + 1) * per_head]:
                (emit_P1_gu if kind == "gu" else emit_P1_pp)(i)
            emit_P3_head(h)
        P.barrier()
    if "P4" in phases:
        for tb in range(4):
            emit_P4_tb(tb)
        P.barrier()
    if "P5" in phases:
        for tb in range(4):
            emit_P5a_tb(tb)
        P.barrier()
        for gt in range(NSLOT):
            emit_P5b_prep(gt)
        for sb_i in range(4):
            emit_P5b_proj(sb_i)
    if TRUNC:
        P.ops = P.ops[:TRUNC]
    fence_reads = (["out_w"] if "P5" in phases else []) + sorted(debug_outputs_res(debug_outputs))
    P.add("gpsimd", "FENCE", fence_reads, [], dma=False, tag="fence")

    DMA_POOL = 24
    counts = P.resolve(dma_pool=DMA_POOL)

    sems = {}
    with stack:
        for e in ENGINES:
            sems[(e, -1)] = stack.enter_context(nc.semaphore(f"sem_{e}"))
        for q in ("sync", "gpsimd"):
            for s in range(DMA_POOL):
                sems[(q, s)] = stack.enter_context(nc.semaphore(f"dq_{q}_{s}"))
        fence_sem = stack.enter_context(nc.semaphore("fence_sem"))
        block = stack.enter_context(nc.Block())
        by_engine = {e: [] for e in ENGINES}
        for o in P.ops:
            by_engine[o.engine].append(o)

        def make_body(ename):
            def body(eng):
                for o in by_engine[ename]:
                    for (ws, wts) in o.waits:
                        eng.wait_ge(sems[ws], wts)
                    if o.is_barrier:
                        eng.sem_inc(sems[o.sem], 1)
                    elif o.fn == "FENCE":
                        eng.sem_inc(fence_sem, 1)
                    else:
                        inst = o.fn(eng)
                        if o.inc is not None:
                            inst.then_inc(sems[o.sem], o.inc)
            return body

        for ename in ENGINES:
            getattr(block, ename)(make_body(ename))

    return nc, counts, len(P.ops)


def debug_outputs_res(names):
    m = {"k_fm": "k_fm_w", "v_tok": "v_tok_w", "q_fm": "q_fm_w", "y_hi": "y_hi_w",
         "y_lo": "y_lo_w", "x2f": "x2f_w", "x2fm_hi": "x2fm_hi_w", "x2fm_lo": "x2fm_lo_w",
         "h_hi": "h_hi_w", "h_lo": "h_lo_w", "hn_hi": "hn_hi_w", "hn_lo": "hn_lo_w",
         "guT": "guT_w", "ppT": "ppT_w", "wq_hi": "wq_hi", "wq_lo": "wq_lo",
         "wu_hi": "wu_hi", "wu_lo": "wu_lo", "wd_hi": "wd_hi", "wd_lo": "wd_lo",
         "wp_hi": "wpA_hi", "wp_lo": "wpA_lo"}
    return [m[n] for n in names if n in m]


_CACHE = {}


def _get_program(debug_outputs=()):
    key = tuple(sorted(debug_outputs))
    if key not in _CACHE:
        _CACHE[key] = build(debug_outputs)
    return _CACHE[key]


def make_core_inputs(core, x, wq, wdn, wup, wpj, qgain, wgu, wpr):
    b, p = core // 2, core & 1
    cosk, sink = host_tables()
    xb = np.ascontiguousarray(x[b])
    x_own = np.ascontiguousarray(xb.reshape(32, 128, D)[p::2].reshape(NQ, D))
    cq = np.ascontiguousarray(cosk.reshape(32, 128, 32)[p::2].reshape(NQ, 32))
    sq = np.ascontiguousarray(sink.reshape(32, 128, 32)[p::2].reshape(NQ, 32))
    return {
        "x_kv": xb,
        "x_own": x_own,
        "w_q_krope": wq,
        "w_kv_down": wdn,
        "w_kv_up": wup,
        "w_proj": wpj,
        "q_gain": qgain.reshape(1, H).astype(np.float32),
        "w_gate_up": wgu,
        "w_ffn_proj": wpr,
        "cosk": cosk, "sink": sink, "cosq": cq, "sinq": sq,
        "maskt": host_mask(p),
        "idn16": np.eye(128, dtype=np.float32).astype(BF),
        "idn32": np.eye(128, dtype=np.float32),
        "ones1": np.ones((1, 128), np.float32),
        "epsil": np.tile(np.array([[1e-6, 1e-5, np.finfo(np.float32).eps, 0.0]],
                                  np.float32), (128, 1)),
    }


def kernel(L0_attn_x, L0_attn_w_q_krope, L0_attn_w_kv_down, L0_attn_w_kv_up,
           L0_attn_w_proj, L0_attn_q_gain, L0_ffn_w_gate_up, L0_ffn_w_proj):
    nc, counts, n_ops = _get_program()
    in_maps = [make_core_inputs(c, L0_attn_x, L0_attn_w_q_krope, L0_attn_w_kv_down,
                                L0_attn_w_kv_up, L0_attn_w_proj, L0_attn_q_gain,
                                L0_ffn_w_gate_up, L0_ffn_w_proj)
               for c in range(8)]
    res = run_bass_kernel_spmd(nc, in_maps, core_ids=list(range(8)))
    outf = np.empty((B_, S, D), np.float32)
    for c in range(8):
        b, p = c // 2, c & 1
        outf[b].reshape(32, 128, D)[p::2] = res.results[c]["out_rows"].reshape(NSLOT, 128, D)
    return outf


# version 03
# speedup vs baseline: 1.0911x; 1.0911x over previous
import contextlib
import numpy as np
import ml_dtypes
import concourse.bass as bass
import concourse.mybir as mybir
from concourse.bass_utils import run_bass_kernel_spmd
from concourse.alu_op_type import AluOpType

F32 = mybir.dt.float32
BF16 = mybir.dt.bfloat16
AF = mybir.ActivationFunctionType
BF = ml_dtypes.bfloat16
X = mybir.AxisListType.X

B_, S, D = 4, 4096, 2048
H, HD, ROPE, NOPE, LAT = 16, 128, 64, 64, 512
FFH = 8192
NQ = 2048
NSLOT = 16
INV_SQRT_HD = float(np.float32(1.0) / np.float32(np.sqrt(np.float32(128.0))))
SQRT_HD = float(np.sqrt(128.0))
TERN_B = float(0.5 + 2.0 ** -10)
F32EPS = float(np.finfo(np.float32).eps)

ENGINES = ("tensor", "vector", "scalar", "gpsimd", "sync")
P2_KV_STRIPS = 8
P2_Q_STRIPS = 4
P2_STAGE = "all"
TRUNC = 0


class Op:
    __slots__ = ("engine", "fn", "reads", "writes", "is_dma", "is_barrier",
                 "waits", "inc", "ts", "tag", "sem")

    def __init__(self, engine, fn, reads, writes, is_dma, tag=""):
        self.engine = engine
        self.fn = fn
        self.reads = tuple(reads)
        self.writes = tuple(writes)
        self.is_dma = is_dma
        self.is_barrier = False
        self.waits = []
        self.inc = None
        self.ts = 0
        self.sem = None
        self.tag = tag


class Prog:
    def __init__(self):
        self.ops = []

    def add(self, engine, fn, reads=(), writes=(), dma=False, tag=""):
        op = Op(engine, fn, reads, writes, dma, tag)
        self.ops.append(op)
        return op

    def barrier(self):
        op = Op("BARRIER", None, (), (), False, "barrier")
        op.is_barrier = True
        self.ops.append(op)

    def resolve(self, dma_pool=24):
        deps = {}
        last_writer = {}
        readers = {}
        barrier_frontier = []
        expanded = []

        def skip_edge(a, b, r):
            return (a.is_dma and b.is_dma and (r == "in" or r.endswith("_w")))

        for op in self.ops:
            if op.is_barrier:
                d = set(last_writer.values())
                for rl in readers.values():
                    d.update(rl)
                d.update(barrier_frontier)
                subs = []
                for e in ENGINES:
                    sub = Op(e, None, (), (), False, "barrier")
                    sub.is_barrier = True
                    deps[id(sub)] = set(d)
                    subs.append(sub)
                    expanded.append(sub)
                barrier_frontier = subs
                last_writer = {}
                readers = {}
                continue
            d = set(barrier_frontier)
            for r in op.reads:
                w = last_writer.get(r)
                if w is not None and not skip_edge(w, op, r):
                    d.add(w)
                readers.setdefault(r, []).append(op)
            for r in op.writes:
                w = last_writer.get(r)
                if w is not None and not skip_edge(w, op, r):
                    d.add(w)
                for rd in readers.get(r, ()):
                    if rd is not op and not skip_edge(rd, op, r):
                        d.add(rd)
                last_writer[r] = op
                readers[r] = []
            deps[id(op)] = d
            expanded.append(op)
        self.ops = expanded

        UNSAFE_SELF = {"vector", "scalar", "gpsimd"}
        needed = set()
        for op in self.ops:
            for dep in deps[id(op)]:
                if dep.engine != op.engine or dep.is_dma or dep.engine in UNSAFE_SELF:
                    needed.add(id(dep))

        eng_count = {}
        dma_idx = {}
        for op in self.ops:
            if op.is_dma:
                i = dma_idx.get(op.engine, 0)
                dma_idx[op.engine] = i + 1
                slot = i % dma_pool
                use = i // dma_pool + 1
                op.sem = (op.engine, slot)
                op.inc = 16
                op.ts = 16 * use
                if use > 1:
                    op.waits.append((op.sem, 16 * (use - 1)))
            elif id(op) in needed or op.is_barrier:
                c = eng_count.get(op.engine, 0) + 1
                eng_count[op.engine] = c
                op.sem = (op.engine, -1)
                op.inc = 1
                op.ts = c
            else:
                op.sem = None
                op.inc = None
                op.ts = 0
        frontier = {}
        for op in self.ops:
            fr = frontier.setdefault(op.engine, {})
            want = {}
            for dep in deps[id(op)]:
                if (dep.engine == op.engine and not dep.is_dma
                        and dep.engine not in UNSAFE_SELF):
                    continue
                if dep.sem is None:
                    continue
                if dep.ts > want.get(dep.sem, 0):
                    want[dep.sem] = dep.ts
            for sem, ts in sorted(want.items()):
                if fr.get(sem, 0) >= ts:
                    continue
                op.waits.append((sem, ts))
                fr[sem] = ts
        self.dma_idx = dma_idx
        return eng_count


def host_tables():
    inv = (np.float32(10000.0) ** (-(np.arange(0, ROPE, 2).astype(np.float32)) / np.float32(ROPE))).astype(np.float32)
    t = np.arange(S, dtype=np.float32)
    fr = (t[:, None] * inv[None, :]).astype(np.float32)
    return np.cos(fr).astype(np.float32), np.sin(fr).astype(np.float32)


def host_mask(p):
    r = np.arange(128)[:, None]
    c = np.arange(256)[None, :]
    return np.where(c <= p * 128 + r, np.float32(0.0), np.float32(-1e30)).astype(BF)


def strips_for_slot(j):
    blocks = 2 * j + 2
    tail = 2 if blocks % 4 == 2 else 4
    full = (blocks - tail) // 4
    return [(i * 512, 512) for i in range(full)] + [(full * 512, tail * 128)]


class Arena:

    def __init__(self, handle, n_elems):
        self.h = handle
        self.n = n_elems
        self.off = 0

    def base(self, off):
        self.off = off
        return self

    def take(self, shape, dt):
        n = shape[1] * (2 if dt == F32 else 1)
        off = (self.off + 63) & ~63
        self.off = off + n
        assert self.off <= self.n, f"arena overflow at {self.off} > {self.n}"
        sl = self.h[0:shape[0], off:off + n]
        if dt != BF16:
            sl = sl.bitcast(dt)
        return sl


def build(debug_outputs=(), phases=('P0','P1b','P1','P2','P3','P4','P5')):
    nc = bass.Bass("TRN2", target_bir_lowering=False)
    P = Prog()
    stack = contextlib.ExitStack()
    debug_outputs = set(debug_outputs)

    def inp(name, shape, dt):
        return nc.dram_tensor(name, list(shape), dt, kind="ExternalInput").ap()

    def outp(name, shape, dt):
        return nc.dram_tensor(name, list(shape), dt, kind="ExternalOutput").ap()

    def scratch(name, shape, dt):
        if name in debug_outputs:
            return nc.dram_tensor(name, list(shape), dt, kind="ExternalOutput").ap()
        return nc.dram_tensor(name, list(shape), dt).ap()

    def sbuf(name, shape, dt):
        return stack.enter_context(nc.sbuf_tensor(name, list(shape), dt))

    dma_ = lambda q, o, i, r, w, tag="": P.add(
        q, (lambda e, o=o, i=i: e.dma_start(out=o, in_=i)), r, w, dma=True, tag=tag)
    op = lambda eng, fn, r, w, tag="": P.add(eng, fn, r, w, dma=False, tag=tag)

    x_kv = inp("x_kv", (S, D), F32)
    x_own = inp("x_own", (NQ, D), F32)
    wq = inp("w_q_krope", (D, 3072), F32)
    wd = inp("w_kv_down", (D, LAT), F32)
    wu = inp("w_kv_up", (LAT, 3072), F32)
    wp = inp("w_proj", (D, D), F32)
    qg = inp("q_gain", (1, H), F32)
    wgu = inp("w_gate_up", (2 * FFH, D), F32)
    wpr = inp("w_ffn_proj", (D, FFH), F32)
    cosk = inp("cosk", (S, 32), F32)
    sink = inp("sink", (S, 32), F32)
    cosq = inp("cosq", (NQ, 32), F32)
    sinq = inp("sinq", (NQ, 32), F32)
    maskt = inp("maskt", (128, 256), BF16)
    idn16 = inp("idn16", (128, 128), BF16)
    idn32 = inp("idn32", (128, 128), F32)
    ones1 = inp("ones1", (1, 128), F32)
    epsil = inp("epsil", (128, 4), F32)
    out = outp("out_rows", (NQ, D), F32)

    wq_hi = scratch("wq_hi", (D, 3072), BF16)
    wq_lo = scratch("wq_lo", (D, 3072), BF16)
    wd_hi = scratch("wd_hi", (D, LAT), BF16)
    wd_lo = scratch("wd_lo", (D, LAT), BF16)
    wu_hi = scratch("wu_hi", (LAT, 3072), BF16)
    wu_lo = scratch("wu_lo", (LAT, 3072), BF16)
    wp_hi = scratch("wp_hi", (D, D), BF16)
    wp_lo = scratch("wp_lo", (D, D), BF16)
    guT = scratch("guT", (D, 2 * FFH), BF16)
    ppT = scratch("ppT", (FFH, D), BF16)
    q_fm = scratch("q_fm", (H, HD, NQ), F32)
    k_fm = scratch("k_fm", (H, HD, S), F32)
    v_tok = scratch("v_tok", (H, S, HD), F32)
    y_hi = scratch("y_hi", (H, HD, NQ), BF16)
    y_lo = scratch("y_lo", (H, HD, NQ), BF16)
    x2f = scratch("x2f", (NQ, D), F32)
    x2fm_hi = scratch("x2fm_hi", (D, NQ), BF16)
    x2fm_lo = scratch("x2fm_lo", (D, NQ), BF16)
    h_hi = scratch("h_hi", (NQ, FFH), BF16)
    h_lo = scratch("h_lo", (NQ, FFH), BF16)
    hn_hi = scratch("hn_hi", (FFH, NQ), BF16)
    hn_lo = scratch("hn_lo", (FFH, NQ), BF16)

    idn16_sb = sbuf("idn16_sb", (128, 128), BF16)
    idn32_sb = sbuf("idn32_sb", (128, 128), F32)
    mask_sb = sbuf("mask_sb", (128, 256), BF16)
    ones_sb = sbuf("ones_sb", (1, 128), F32)
    qg_sb = sbuf("qg_sb", (1, H), F32)
    qg_tmp = sbuf("qg_tmp", (1, H), F32)
    gain_b = sbuf("gain_b", (128, H), F32)
    negc_b = sbuf("negc_b", (128, H), F32)
    rstat = sbuf("rstat", (128, 16), F32)
    rstat2 = sbuf("rstat2", (128, 16), F32)
    rstd_ffn = sbuf("rstd_ffn", (128, NSLOT), F32)
    hsq_acc = sbuf("hsq_acc", (128, 16), F32)
    rs_sb = sbuf("rs_sb", (128, 16), F32)
    rsum = sbuf("rsum", (128, 2), F32)
    rrow = sbuf("rrow", (1, 128), F32)
    R_sb = sbuf("R_sb", (128, 128), F32)
    yf_sb = sbuf("yf_sb", (128, 128), F32)
    yhi_sb = sbuf("yhi_sb", (128, 128), BF16)
    ylo_sb = sbuf("ylo_sb", (128, 128), BF16)
    eps_sb = sbuf("eps_sb", (128, 4), F32)
    qsc = sbuf("qsc", (128, 32), BF16)
    qrs = sbuf("qrs", (128, 32), F32)
    costl = sbuf("costl", (128, 32), F32)
    sintl = sbuf("sintl", (128, 32), F32)
    ropet = [sbuf(f"ropet{i}", (128, 32), F32) for i in range(4)]

    AR_N = 88064
    AR = sbuf("AR", (128, AR_N), BF16)
    ar = Arena(AR[:], AR_N)

    ar.base(0)
    wf = [ar.take((128, 3072), F32) for _ in range(2)]
    w16p = [ar.take((128, 3072), BF16) for _ in range(2)]
    wlop = [ar.take((128, 3072), BF16) for _ in range(2)]

    ar.base(0)
    qw16 = [ar.take((128, 2048), BF16) for _ in range(2)]
    qm32 = ar.take((128, 2048), F32)
    qaux = ar.take((128, 2048), BF16)
    qq_t = ar.take((128, 2048), BF16)
    stg_gu = [ar.take((128, 512), BF16) for _ in range(16)]
    stg_pp = [ar.take((128, 2048), BF16) for _ in range(4)]
    QUANT_END = ar.off

    ar.base(0)
    xtm = [ar.take((128, D), F32) for _ in range(2)]
    xh_t = [ar.take((128, D), BF16) for _ in range(4)]
    xl_t = [ar.take((128, D), BF16) for _ in range(4)]
    xfm_hi = [ar.take((128, 512), BF16) for _ in range(16)]
    xfm_lo = [ar.take((128, 512), BF16) for _ in range(16)]
    wmov = [ar.take((128, 512), BF16) for _ in range(4)]
    praw = [ar.take((128, 512), F32) for _ in range(2)]
    psq = ar.take((128, 512), F32)
    pnrm = ar.take((128, 512), F32)
    ckv_hi = [ar.take((128, 512), BF16) for _ in range(4)]
    ckv_lo = [ar.take((128, 512), BF16) for _ in range(4)]
    ckv_f = ar.take((128, 512), F32)
    ckv_h2 = ar.take((128, 512), BF16)
    ckv_l2 = ar.take((128, 512), BF16)
    kfm_sb = [ar.take((128, 512), F32) for _ in range(H)]
    P2_END = ar.off

    ar.base(30720)
    q_sb = ar.take((128, NQ), F32)
    k_sb = [ar.take((128, 512), F32) for _ in range(2)]
    v_sb = [ar.take((128, 512), F32) for _ in range(2)]
    probs_sb = [ar.take((128, 512), F32) for _ in range(2)]
    pT_sb = [ar.take((128, 512), F32) for _ in range(2)]
    P3_END = ar.off

    ar.base(0)
    ytb = ar.take((128, 16384), BF16)
    o1_sb = [ar.take((128, D), F32) for _ in range(4)]
    xres = [ar.take((128, D), F32) for _ in range(2)]
    s1_sb = ar.take((128, D), F32)
    x2f_sb = ar.take((128, D), F32)
    x2hi_sb = ar.take((128, D), BF16)
    x2lo_sb = ar.take((128, D), BF16)
    x2stg = [ar.take((128, D), BF16) for _ in range(2)]
    psq4 = ar.take((128, 512), F32)
    wmv4 = [ar.take((128, 512), BF16) for _ in range(4)]
    P4_END = ar.off

    ar.base(0)
    ytb5 = ar.take((128, 16384), BF16)
    gmov = [ar.take((128, 512), BF16) for _ in range(3)]
    sg_sb = [ar.take((128, 512), F32) for _ in range(4)]
    hf_sb = ar.take((128, 512), F32)
    hhi_sb = ar.take((128, 512), BF16)
    hlo_sb = ar.take((128, 512), BF16)
    hsqd = ar.take((128, 512), F32)
    P5A_END = ar.off

    ar.base(0)
    hl_in = [ar.take((128, 4096), BF16) for _ in range(2)]
    hnf_sb = ar.take((128, 4096), F32)
    hn16_sb = [ar.take((128, 4096), BF16) for _ in range(2)]
    hnstg = [ar.take((128, 4096), BF16) for _ in range(2)]
    P5BP_END = ar.off

    ar.base(36864)
    hnq = [ar.take((128, 512), BF16) for _ in range(32)]
    pst = [ar.take((128, 512), BF16) for _ in range(4)]
    o2acc = [ar.take((128, 512), F32) for _ in range(16)]
    outt = ar.take((128, D), F32)
    xres5 = ar.take((128, D), F32)
    psq5 = ar.take((128, 512), F32)
    P5BJ_END = ar.off
    assert max(P2_END, P3_END, P4_END, P5A_END, P5BP_END, P5BJ_END) <= AR_N

    bk = [stack.enter_context(nc.psum_tensor(f"bk{i}", [128, 512], F32)) for i in range(6)]
    bkTT = [stack.enter_context(nc.psum_tensor(f"bkTT{i}", [128, 1024], BF16)) for i in range(2)]

    def bkT_ap(half, c0, c1):
        return bkTT[half][:, c0:c0 + (c1 - c0)]

    OUT_RES = ["out_w"]

    def emit_P0():
        dma_("sync", idn16_sb[:], idn16[:, :], ["in"], ["idn16_sb"])
        dma_("sync", idn32_sb[:], idn32[:, :], ["in"], ["idn32_sb"])
        dma_("sync", mask_sb[:], maskt[:, :], ["in"], ["mask_sb"])
        dma_("sync", ones_sb[:], ones1[:, :], ["in"], ["ones_sb"])
        dma_("sync", qg_sb[:], qg[:, :], ["in"], ["qg_sb"])
        dma_("sync", eps_sb[:], epsil[:, :], ["in"], ["eps_sb"])
        op("tensor", lambda e: e.matmul(bk[0][:, 0:H], ones_sb[:], qg_sb[:], start=True, stop=True),
           ["ones_sb", "qg_sb"], ["bk0"])
        op("vector", lambda e: e.tensor_copy(gain_b[:], bk[0][:, 0:H]), ["bk0"], ["gain_b"])
        op("scalar", lambda e: e.activation(qg_tmp[:], qg_sb[:], AF.Abs), ["qg_sb"], ["qg_tmp"])
        op("scalar", lambda e: e.mul(qg_tmp[:], qg_tmp[:], -SQRT_HD), ["qg_tmp"], ["qg_tmp"])
        op("tensor", lambda e: e.matmul(bk[0][:, 0:H], ones_sb[:], qg_tmp[:], start=True, stop=True),
           ["ones_sb", "qg_tmp"], ["bk0"])
        op("vector", lambda e: e.tensor_copy(negc_b[:], bk[0][:, 0:H]), ["bk0"], ["negc_b"])

    def emit_P1b():
        jobs = [(wq, wq_hi, wq_lo, D, 3072, "wq"), (wd, wd_hi, wd_lo, D, LAT, "wd"),
                (wu, wu_hi, wu_lo, LAT, 3072, "wu"), (wp, wp_hi, wp_lo, D, D, "wpA")]
        for (w_in, o_hi, o_lo, RR, CC, nm) in jobs:
            for r in range(RR // 128):
                i = r % 2
                rows = slice(r * 128, (r + 1) * 128)
                dma_("sync", wf[i][:, 0:CC], w_in[rows, :], ["in"], [f"wf{i}"])
                dma_("gpsimd", w16p[i][:, 0:CC], w_in[rows, :], ["in"], [f"w16p{i}"])
                op("vector",
                   lambda e, i=i, CC=CC: e.tensor_tensor(wlop[i][:, 0:CC], wf[i][:, 0:CC],
                                                         w16p[i][:, 0:CC], op=AluOpType.subtract),
                   [f"wf{i}", f"w16p{i}"], [f"wlop{i}"])
                dma_("gpsimd", o_hi[rows, :], w16p[i][:, 0:CC], [f"w16p{i}"], [nm + "_hi"])
                dma_("gpsimd", o_lo[rows, :], wlop[i][:, 0:CC], [f"wlop{i}"], [nm + "_lo"])

    def _quant_tile(i, CC):
        G = CC // 64
        op("vector",
           lambda e, i=i, CC=CC, G=G: e.tensor_reduce(
               qrs[:, 0:G], qw16[i][:, 0:CC].rearrange("p (g k) -> p g k", k=64),
               axis=X, op=AluOpType.add, apply_absolute_value=True),
           [f"qw16{i}"], ["qrs"])
        op("vector",
           lambda e, G=G: e.tensor_scalar(qsc[:, 0:G], qrs[:, 0:G], 1.0 / 64.0, 1e-8,
                                          op0=AluOpType.mult, op1=AluOpType.max),
           ["qrs"], ["qsc"])
        op("vector", lambda e, G=G: e.reciprocal(qrs[:, 0:G], qsc[:, 0:G]), ["qsc"], ["qrs"])
        op("vector",
           lambda e, i=i, CC=CC, G=G: e.tensor_tensor(
               qm32[:, 0:CC].rearrange("p (g k) -> p g k", k=64),
               qw16[i][:, 0:CC].rearrange("p (g k) -> p g k", k=64),
               qrs[:, 0:G].unsqueeze(2).broadcast_to([128, G, 64]),
               op=AluOpType.mult),
           [f"qw16{i}", "qrs"], ["qm32"])
        op("vector",
           lambda e, CC=CC: e.tensor_scalar(qaux[:, 0:CC], qm32[:, 0:CC], -TERN_B, None,
                                            op0=AluOpType.is_lt),
           ["qm32"], ["qaux"])
        op("vector",
           lambda e, CC=CC: e.scalar_tensor_tensor(qq_t[:, 0:CC], qm32[:, 0:CC], TERN_B,
                                                   qaux[:, 0:CC], op0=AluOpType.is_gt,
                                                   op1=AluOpType.subtract),
           ["qm32", "qaux"], ["qq_t"])
        op("vector",
           lambda e, CC=CC, G=G: e.tensor_tensor(
               qq_t[:, 0:CC].rearrange("p (g k) -> p g k", k=64),
               qq_t[:, 0:CC].rearrange("p (g k) -> p g k", k=64),
               qsc[:, 0:G].unsqueeze(2).broadcast_to([128, G, 64]),
               op=AluOpType.mult),
           ["qq_t", "qsc"], ["qq_t"])
        op("vector",
           lambda e, i=i, CC=CC: e.tensor_tensor(qaux[:, 0:CC], qq_t[:, 0:CC],
                                                 qw16[i][:, 0:CC], op=AluOpType.subtract),
           ["qq_t", f"qw16{i}"], ["qaux"])
        op("vector",
           lambda e, i=i, CC=CC: e.tensor_tensor(qw16[i][:, 0:CC], qw16[i][:, 0:CC],
                                                 qaux[:, 0:CC], op=AluOpType.add),
           [f"qw16{i}", "qaux"], [f"qw16{i}"])

    def emit_P1_gu(rb):
        i = rb % 2
        rows = slice(rb * 128, (rb + 1) * 128)
        dma_("gpsimd", qw16[i][:, 0:D], wgu[rows, :], ["in"], [f"qw16{i}"])
        _quant_tile(i, D)
        for c in range(16):
            half = (c % 8) // 4
            sc_ = (c % 4) * 128
            op("tensor",
               lambda e, i=i, c=c, half=half, sc_=sc_: e.transpose(
                   bkTT[half][:, sc_:sc_ + 128],
                   qw16[i][:, c * 128:(c + 1) * 128], idn16_sb[:]),
               [f"qw16{i}", "idn16_sb"], [f"bkT{half}"])
            op("vector",
               lambda e, c=c, rb=rb, half=half, sc_=sc_: e.tensor_copy(
                   stg_gu[c][:, (rb % 4) * 128:(rb % 4 + 1) * 128],
                   bkTT[half][:, sc_:sc_ + 128]),
               [f"bkT{half}"], [f"stg_gu{c}"])
        if rb % 4 == 3:
            for c in range(16):
                dma_("gpsimd", guT[c * 128:(c + 1) * 128, (rb - 3) * 128:(rb + 1) * 128],
                     stg_gu[c][:, :], [f"stg_gu{c}"], ["guT_w"])

    def emit_P1_pp(hb):
        for db in range(16):
            i = db % 2
            rows = slice(db * 128, (db + 1) * 128)
            dma_("gpsimd", qw16[i][:, 0:512], wpr[rows, hb * 512:(hb + 1) * 512],
                 ["in"], [f"qw16{i}"])
            _quant_tile(i, 512)
            for c in range(4):
                half = db % 2
                sc_ = c * 128
                op("tensor",
                   lambda e, i=i, c=c, half=half, sc_=sc_: e.transpose(
                       bkTT[half][:, sc_:sc_ + 128],
                       qw16[i][:, c * 128:(c + 1) * 128], idn16_sb[:]),
                   [f"qw16{i}", "idn16_sb"], [f"bkT{half}"])
                op("vector",
                   lambda e, c=c, db=db, half=half, sc_=sc_: e.tensor_copy(
                       stg_pp[c][:, db * 128:(db + 1) * 128],
                       bkTT[half][:, sc_:sc_ + 128]),
                   [f"bkT{half}"], [f"stg_pp{c}"])
        for c in range(4):
            dma_("gpsimd", ppT[hb * 512 + c * 128: hb * 512 + (c + 1) * 128, :],
                 stg_pp[c][:, :], [f"stg_pp{c}"], ["ppT_w"])

    def load_xfm(src_ap, row0):
        for t in range(4):
            i = t % 2
            rows = slice(row0 + t * 128, row0 + (t + 1) * 128)
            dma_("sync", xtm[i][:], src_ap[rows, :], ["in"], [f"xtm{i}"])
            op("vector", lambda e, i=i, t=t: e.tensor_copy(xh_t[t][:], xtm[i][:]),
               [f"xtm{i}"], [f"xh{t}"])
            op("vector",
               lambda e, i=i, t=t: e.tensor_tensor(xl_t[t][:], xtm[i][:], xh_t[t][:],
                                                   op=AluOpType.subtract),
               [f"xtm{i}", f"xh{t}"], [f"xl{t}"])
        for limb in range(2):
            src_t = xh_t if limb == 0 else xl_t
            pre = "xh" if limb == 0 else "xl"
            dst = xfm_hi if limb == 0 else xfm_lo
            dnm = "xfmh" if limb == 0 else "xfml"
            for dc in range(16):
                half = dc % 2
                for t in range(4):
                    op("tensor",
                       lambda e, src_t=src_t, t=t, dc=dc, half=half: e.transpose(
                           bkTT[half][:, t * 128:(t + 1) * 128],
                           src_t[t][:, dc * 128:(dc + 1) * 128], idn16_sb[:]),
                       [pre + str(t), "idn16_sb"], [f"bkT{half}"])
                op("vector",
                   lambda e, dst=dst, dc=dc, half=half: e.tensor_copy(
                       dst[dc][:, :], bkTT[half][:, 0:512]),
                   [f"bkT{half}"], [f"{dnm}{dc}"])

    def proj_group(whi_ap, wlo_ap, wc0, hires, lores):
        for dc in range(16):
            wi = (dc % 2) * 2
            dma_("sync", wmov[wi][:, :], whi_ap[dc * 128:(dc + 1) * 128, wc0:wc0 + 512],
                 [hires], [f"wmov{wi}"])
            dma_("sync", wmov[wi + 1][:, :], wlo_ap[dc * 128:(dc + 1) * 128, wc0:wc0 + 512],
                 [lores], [f"wmov{wi + 1}"])
            for t in range(4):
                plan = [(xfm_hi, "xfmh", wi), (xfm_lo, "xfml", wi), (xfm_hi, "xfmh", wi + 1)]
                for pi, (xt, xn, wj) in enumerate(plan):
                    st = (dc == 0 and pi == 0)
                    sp = (dc == 15 and pi == 2)
                    op("tensor",
                       lambda e, xt=xt, dc=dc, t=t, wj=wj, st=st, sp=sp: e.matmul(
                           bk[t][:, :], xt[dc][:, t * 128:(t + 1) * 128],
                           wmov[wj][:, :], start=st, stop=sp),
                       [f"{xn}{dc}", f"wmov{wj}"], [f"bk{t}"])

    def kvup_group(g):
        for l in range(4):
            wi = (l % 2) * 2
            dma_("sync", wmov[wi][:, :], wu_hi[l * 128:(l + 1) * 128, g * 512:(g + 1) * 512],
                 ["wu_hi"], [f"wmov{wi}"])
            dma_("sync", wmov[wi + 1][:, :], wu_lo[l * 128:(l + 1) * 128, g * 512:(g + 1) * 512],
                 ["wu_lo"], [f"wmov{wi + 1}"])
            for t in range(4):
                plan = [(ckv_hi, "ckvh", wi), (ckv_lo, "ckvl", wi), (ckv_hi, "ckvh", wi + 1)]
                for pi, (ct, cn, wj) in enumerate(plan):
                    st = (l == 0 and pi == 0)
                    sp = (l == 3 and pi == 2)
                    op("tensor",
                       lambda e, ct=ct, l=l, t=t, wj=wj, st=st, sp=sp: e.matmul(
                           bk[t][:, :], ct[l][:, t * 128:(t + 1) * 128],
                           wmov[wj][:, :], start=st, stop=sp),
                       [f"{cn}{l}", f"wmov{wj}"], [f"bk{t}"])

    def rms_units(t, bank_name, bank_ap, with_rope, cos_ap, sin_ap, row0,
                  head0, gain, units_per_head, dst_rows_fixed=None):
        pr, prn = praw[t % 2], f"praw{t % 2}"
        op("scalar", lambda e: e.copy(pr[:], bank_ap[:, :]), [bank_name], [prn])
        op("scalar", lambda e: e.activation(psq[:], pr[:], AF.Square), [prn], ["psq"])
        op("vector",
           lambda e: e.tensor_reduce(rstat[:, 0:8], psq[:].rearrange("p (g k) -> p g k", k=64),
                                     axis=X, op=AluOpType.add),
           ["psq"], ["rstat"])
        op("scalar",
           lambda e: e.activation(rstat2[:, 0:8], rstat[:, 0:8], AF.Sqrt,
                                  bias=eps_sb[:, 0:1], scale=1.0 / 64.0),
           ["rstat", "eps_sb"], ["rstat2"])
        op("vector", lambda e: e.reciprocal(rstat[:, 0:8], rstat2[:, 0:8]), ["rstat2"], ["rstat"])
        if gain:
            for c in range(4):
                hh = head0 + c
                op("vector",
                   lambda e, c=c, hh=hh: e.tensor_scalar(rstat[:, 2 * c:2 * c + 2],
                                                         rstat[:, 2 * c:2 * c + 2],
                                                         gain_b[:, hh:hh + 1], None,
                                                         op0=AluOpType.mult),
                   ["rstat", "gain_b"], ["rstat"])
        if with_rope:
            dma_("sync", costl[:], cos_ap[row0:row0 + 128, :], ["in"], ["costl"])
            dma_("sync", sintl[:], sin_ap[row0:row0 + 128, :], ["in"], ["sintl"])
        for u in range(8):
            u0 = u * 64
            rope_u = with_rope and (units_per_head == 1 or (u % 2 == 1))
            if rope_u:
                for (di, a_sl, ct, cn) in [(0, slice(u0, u0 + 32), costl, "costl"),
                                           (1, slice(u0 + 32, u0 + 64), sintl, "sintl"),
                                           (2, slice(u0, u0 + 32), sintl, "sintl"),
                                           (3, slice(u0 + 32, u0 + 64), costl, "costl")]:
                    op("vector",
                       lambda e, di=di, a_sl=a_sl, ct=ct, pr=pr: e.tensor_tensor(
                           ropet[di][:], pr[:, a_sl], ct[:], op=AluOpType.mult),
                       [prn, cn], [f"ropet{di}"])
                op("vector",
                   lambda e, u0=u0: e.tensor_tensor(pnrm[:, u0:u0 + 32], ropet[0][:],
                                                    ropet[1][:], op=AluOpType.add),
                   ["ropet0", "ropet1"], ["pnrm"])
                op("vector",
                   lambda e, u0=u0: e.tensor_tensor(pnrm[:, u0 + 32:u0 + 64], ropet[3][:],
                                                    ropet[2][:], op=AluOpType.subtract),
                   ["ropet3", "ropet2"], ["pnrm"])
                src, srcn = pnrm, "pnrm"
            else:
                src, srcn = pr, prn
            op("vector",
               lambda e, u=u, u0=u0, src=src: e.tensor_scalar(
                   pnrm[:, u0:u0 + 64], src[:, u0:u0 + 64], rstat[:, u:u + 1], None,
                   op0=AluOpType.mult),
               [srcn, "rstat"], ["pnrm"])
        for up in range(4):
            u0 = up * 128
            op("tensor",
               lambda e, u0=u0: e.transpose(bk[4][:, 0:128], pnrm[:, u0:u0 + 128],
                                            idn32_sb[:]),
               ["pnrm", "idn32_sb"], ["bk4"])
            if units_per_head == 2:
                head = head0 + up
                op("vector",
                   lambda e, head=head, t=t: e.tensor_copy(
                       kfm_sb[head][:, t * 128:(t + 1) * 128], bk[4][:, 0:128]),
                   ["bk4"], [f"kfmsb{head}"])
            else:
                for half in range(2):
                    head = head0 + 2 * up + half
                    op("vector",
                       lambda e, head=head, half=half, t=t: e.tensor_copy(
                           kfm_sb[head][dst_rows_fixed:dst_rows_fixed + 64,
                                        t * 128:(t + 1) * 128],
                           bk[4][half * 64:(half + 1) * 64, 0:128]),
                       ["bk4"], [f"kfmsb{head}"])

    def emit_P2_kv_strip(s):
        T0 = s * 512
        load_xfm(x_kv, T0)
        if P2_STAGE == "xfm":
            return
        for g, (whi_ap, wlo_ap, wc0, hr, lr) in enumerate([
                (wq_hi, wq_lo, 2048, "wq_hi", "wq_lo"),
                (wq_hi, wq_lo, 2560, "wq_hi", "wq_lo"),
                (wd_hi, wd_lo, 0, "wd_hi", "wd_lo")]):
            proj_group(whi_ap, wlo_ap, wc0, hr, lr)
            if P2_STAGE == "proj":
                continue
            for t in range(4):
                if g < 2:
                    rms_units(t, f"bk{t}", bk[t], True, cosk, sink, T0 + t * 128,
                              g * 8, False, 1, dst_rows_fixed=64)
                else:
                    pr, prn = praw[t % 2], f"praw{t % 2}"
                    op("scalar", lambda e, t=t, pr=pr: e.copy(pr[:], bk[t][:, :]),
                       [f"bk{t}"], [prn])
                    op("scalar", lambda e, pr=pr: e.activation(psq[:], pr[:], AF.Square),
                       [prn], ["psq"])
                    op("vector",
                       lambda e: e.tensor_reduce(rstat[:, 0:1], psq[:], axis=X,
                                                 op=AluOpType.add),
                       ["psq"], ["rstat"])
                    op("scalar",
                       lambda e: e.activation(rstat2[:, 0:1], rstat[:, 0:1], AF.Sqrt,
                                              bias=eps_sb[:, 0:1], scale=1.0 / 512.0),
                       ["rstat", "eps_sb"], ["rstat2"])
                    op("vector", lambda e: e.reciprocal(rstat[:, 0:1], rstat2[:, 0:1]),
                       ["rstat2"], ["rstat"])
                    op("vector",
                       lambda e, pr=pr: e.tensor_scalar(ckv_f[:], pr[:], rstat[:, 0:1],
                                                        None, op0=AluOpType.mult),
                       [prn, "rstat"], ["ckvf"])
                    op("vector", lambda e: e.tensor_copy(ckv_h2[:], ckv_f[:]),
                       ["ckvf"], ["ckvh2"])
                    op("vector",
                       lambda e: e.tensor_tensor(ckv_l2[:], ckv_f[:], ckv_h2[:],
                                                 op=AluOpType.subtract),
                       ["ckvf", "ckvh2"], ["ckvl2"])
                    for limb in range(2):
                        srct = ckv_h2 if limb == 0 else ckv_l2
                        srcn = "ckvh2" if limb == 0 else "ckvl2"
                        dstt = ckv_hi if limb == 0 else ckv_lo
                        dnm = "ckvh" if limb == 0 else "ckvl"
                        for l in range(4):
                            half = l % 2
                            op("tensor",
                               lambda e, srct=srct, l=l, half=half: e.transpose(
                                   bkTT[half][:, 0:128],
                                   srct[:, l * 128:(l + 1) * 128], idn16_sb[:]),
                               [srcn, "idn16_sb"], [f"bkT{half}"])
                            op("vector",
                               lambda e, dstt=dstt, l=l, t=t, half=half: e.tensor_copy(
                                   dstt[l][:, t * 128:(t + 1) * 128],
                                   bkTT[half][:, 0:128]),
                               [f"bkT{half}"], [f"{dnm}{l}"])
        if P2_STAGE in ("xfm", "proj", "post"):
            for head in range(H):
                dma_("gpsimd", k_fm[head, :, T0:T0 + 512], kfm_sb[head][:, :],
                     [f"kfmsb{head}"], ["k_fm_w"])
            return
        for g in range(6):
            kvup_group(g)
            for t in range(4):
                if g < 2:
                    rms_units(t, f"bk{t}", bk[t], False, None, None, 0,
                              g * 8, False, 1, dst_rows_fixed=0)
                else:
                    pr, prn = praw[t % 2], f"praw{t % 2}"
                    op("scalar", lambda e, t=t, pr=pr: e.copy(pr[:], bk[t][:, :]),
                       [f"bk{t}"], [prn])
                    for c in range(4):
                        head = (g - 2) * 4 + c
                        dma_("gpsimd", v_tok[head, T0 + t * 128: T0 + (t + 1) * 128, :],
                             pr[:, c * 128:(c + 1) * 128], [prn], ["v_tok_w"])
        for head in range(H):
            dma_("gpsimd", k_fm[head, :, T0:T0 + 512], kfm_sb[head][:, :],
                 [f"kfmsb{head}"], ["k_fm_w"])

    def emit_P2_q_strip(qs):
        T0 = qs * 512
        load_xfm(x_own, T0)
        for g in range(4):
            proj_group(wq_hi, wq_lo, g * 512, "wq_hi", "wq_lo")
            for t in range(4):
                rms_units(t, f"bk{t}", bk[t], True, cosq, sinq, T0 + t * 128,
                          g * 4, True, 2)
        for head in range(H):
            dma_("gpsimd", q_fm[head, :, T0:T0 + 512], kfm_sb[head][:, :],
                 [f"kfmsb{head}"], ["q_fm_w"])

    def emit_P3_head(h):
        dma_("sync", q_sb[:], q_fm[h, :, :], ["q_fm_w"], ["q_sb"])
        for j in range(NSLOT):
            strips = strips_for_slot(j)
            nst = len(strips)

            def qk_exp(si):
                k0, kw = strips[si]
                bi = si % 2
                sc = 2 + bi
                last = (si == nst - 1)
                dma_("sync", k_sb[bi][:, 0:kw], k_fm[h, :, k0:k0 + kw], ["k_fm_w"],
                     [f"k_sb{bi}"])
                dma_("sync", v_sb[bi][:, 0:kw].rearrange("p (c t) -> p c t", t=128),
                     v_tok[h, k0:k0 + kw, :].rearrange("(c p) t -> p c t", p=128),
                     ["v_tok_w"], [f"v_sb{bi}"])
                if last:
                    op("tensor",
                       lambda e, sc=sc, kw=kw: e.matmul(bk[sc][:, kw - 256:kw], idn16_sb[:],
                                                        mask_sb[:], start=True, stop=False),
                       ["idn16_sb", "mask_sb"], [f"bk{sc}"])
                op("tensor",
                   lambda e, sc=sc, j=j, kw=kw, bi=bi, st=(not last): e.matmul(
                       bk[sc][:, 0:kw], q_sb[:, j * 128:(j + 1) * 128],
                       k_sb[bi][:, 0:kw], start=st, stop=True),
                   ["q_sb", f"k_sb{bi}"], [f"bk{sc}"])
                op("scalar",
                   lambda e, sc=sc, kw=kw, bi=bi, h=h, si=si: e.activation(
                       probs_sb[bi][:, 0:kw], bk[sc][:, 0:kw], AF.Exp,
                       bias=negc_b[:, h:h + 1], scale=INV_SQRT_HD,
                       accum_out=rs_sb[:, si:si + 1]),
                   [f"bk{sc}", "negc_b"], [f"probs{bi}", "rs_sb"])

            def t_pv(si):
                k0, kw = strips[si]
                bi = si % 2
                last = (si == nst - 1)
                for c in range(kw // 128):
                    op("tensor",
                       lambda e, bi=bi, c=c: e.transpose(
                           bk[4][:, c * 128:(c + 1) * 128],
                           probs_sb[bi][:, c * 128:(c + 1) * 128], idn32_sb[:]),
                       [f"probs{bi}", "idn32_sb"], ["bk4"])
                eng = "vector" if si % 2 == 0 else "scalar"
                if eng == "vector":
                    op("vector",
                       lambda e, bi=bi, kw=kw: e.tensor_copy(pT_sb[bi][:, 0:kw],
                                                             bk[4][:, 0:kw]),
                       ["bk4"], [f"pT{bi}"])
                else:
                    op("scalar",
                       lambda e, bi=bi, kw=kw: e.copy(pT_sb[bi][:, 0:kw], bk[4][:, 0:kw]),
                       ["bk4"], [f"pT{bi}"])
                for c in range(kw // 128):
                    st = (si == 0 and c == 0)
                    sp = (last and c == kw // 128 - 1)
                    op("tensor",
                       lambda e, bi=bi, c=c, st=st, sp=sp: e.matmul(
                           bk[0][:, 0:128], v_sb[bi][:, c * 128:(c + 1) * 128],
                           pT_sb[bi][:, c * 128:(c + 1) * 128], start=st, stop=sp),
                       [f"v_sb{bi}", f"pT{bi}"], ["bk0"])

            for si in range(nst):
                qk_exp(si)
                if si >= 1:
                    t_pv(si - 1)
            t_pv(nst - 1)
            op("vector",
               lambda e, nst=nst: e.tensor_reduce(rsum[:, 0:1], rs_sb[:, 0:nst], axis=X,
                                                  op=AluOpType.add),
               ["rs_sb"], ["rsum"])
            op("vector", lambda e: e.reciprocal(rsum[:, 1:2], rsum[:, 0:1]), ["rsum"], ["rsum"])
            op("tensor",
               lambda e: e.transpose(bk[5][0:1, 0:128], rsum[:, 1:2], idn32_sb[:]),
               ["rsum", "idn32_sb"], ["bk5"])
            op("vector", lambda e: e.tensor_copy(rrow[:, :], bk[5][0:1, 0:128]),
               ["bk5"], ["rrow"])
            op("tensor",
               lambda e: e.matmul(bk[5][:, 0:128], ones_sb[:], rrow[:, :],
                                  start=True, stop=True),
               ["ones_sb", "rrow"], ["bk5"])
            op("vector", lambda e: e.tensor_copy(R_sb[:], bk[5][:, 0:128]), ["bk5"], ["R_sb"])
            op("vector",
               lambda e: e.tensor_tensor(yf_sb[:], bk[0][:, 0:128], R_sb[:],
                                         op=AluOpType.mult),
               ["bk0", "R_sb"], ["yf_sb"])
            op("vector", lambda e: e.tensor_copy(yhi_sb[:], yf_sb[:]), ["yf_sb"], ["yhi_sb"])
            op("vector",
               lambda e: e.tensor_tensor(ylo_sb[:], yf_sb[:], yhi_sb[:],
                                         op=AluOpType.subtract),
               ["yf_sb", "yhi_sb"], ["ylo_sb"])
            dma_("gpsimd", y_hi[h, :, j * 128:(j + 1) * 128], yhi_sb[:], ["yhi_sb"], ["y_hi_w"])
            dma_("gpsimd", y_lo[h, :, j * 128:(j + 1) * 128], ylo_sb[:], ["ylo_sb"], ["y_lo_w"])

    def emit_P4_tb(tb):
        t0 = tb * 4
        for fc in range(H):
            for limb, (src, nm) in enumerate([(y_hi, "y_hi_w"), (y_lo, "y_lo_w")]):
                dma_("sync", ytb[:, fc * 1024 + limb * 512: fc * 1024 + (limb + 1) * 512],
                     src[fc, :, t0 * 128:(t0 + 4) * 128], [nm], ["ytb"])
        for dgrp in range(4):
            for fc in range(H):
                wi = (fc % 2) * 2
                dma_("sync", wmv4[wi][:, :],
                     wp_hi[fc * 128:(fc + 1) * 128, dgrp * 512:(dgrp + 1) * 512],
                     ["wpA_hi"], [f"wmv4{wi}"])
                dma_("sync", wmv4[wi + 1][:, :],
                     wp_lo[fc * 128:(fc + 1) * 128, dgrp * 512:(dgrp + 1) * 512],
                     ["wpA_lo"], [f"wmv4{wi + 1}"])
                for t in range(4):
                    yh0 = fc * 1024 + t * 128
                    yl0 = fc * 1024 + 512 + t * 128
                    for pi, (y0, wj) in enumerate([(yh0, wi), (yl0, wi), (yh0, wi + 1)]):
                        st = (fc == 0 and pi == 0)
                        sp = (fc == 15 and pi == 2)
                        op("tensor",
                           lambda e, y0=y0, wj=wj, t=t, st=st, sp=sp: e.matmul(
                               bk[t][:, :], ytb[:, y0:y0 + 128], wmv4[wj][:, :],
                               start=st, stop=sp),
                           ["ytb", f"wmv4{wj}"], [f"bk{t}"])
            for t in range(4):
                op("scalar",
                   lambda e, t=t, dgrp=dgrp: e.copy(o1_sb[t][:, dgrp * 512:(dgrp + 1) * 512],
                                                    bk[t][:, :]),
                   [f"bk{t}"], [f"o1sb{t}"])
        for t in range(4):
            gt = t0 + t
            xi = t % 2
            dma_("sync", xres[xi][:], x_own[gt * 128:(gt + 1) * 128, :], ["in"], [f"xres{xi}"])
            op("vector",
               lambda e, xi=xi, t=t: e.scalar_tensor_tensor(
                   s1_sb[:], xres[xi][:], 1.0, o1_sb[t][:], op0=AluOpType.mult,
                   op1=AluOpType.add, accum_out=rstat[:, 0:1]),
               [f"xres{xi}", f"o1sb{t}"], ["s1_sb", "rstat"])
            for c in range(4):
                op("scalar",
                   lambda e, c=c: e.activation(psq4[:], s1_sb[:, c * 512:(c + 1) * 512],
                                               AF.Square, accum_out=rstat2[:, c:c + 1]),
                   ["s1_sb"], ["psq4", "rstat2"])
            op("vector",
               lambda e: e.tensor_reduce(rstat2[:, 4:5], rstat2[:, 0:4], axis=X,
                                         op=AluOpType.add),
               ["rstat2"], ["rstat2"])
            op("vector",
               lambda e: e.tensor_scalar(rstat[:, 1:2], rstat[:, 0:1], 1.0 / 2048.0, None,
                                         op0=AluOpType.mult),
               ["rstat"], ["rstat"])
            op("vector",
               lambda e: e.tensor_tensor(rstat[:, 2:3], rstat[:, 1:2], rstat[:, 1:2],
                                         op=AluOpType.mult),
               ["rstat"], ["rstat"])
            op("vector",
               lambda e: e.scalar_tensor_tensor(rstat[:, 3:4], rstat2[:, 4:5], 1.0 / 2048.0,
                                                rstat[:, 2:3], op0=AluOpType.mult,
                                                op1=AluOpType.subtract),
               ["rstat2", "rstat"], ["rstat"])
            op("scalar",
               lambda e: e.activation(rstat[:, 4:5], rstat[:, 3:4], AF.Sqrt,
                                      bias=eps_sb[:, 1:2]),
               ["rstat", "eps_sb"], ["rstat"])
            op("vector", lambda e: e.reciprocal(rstat[:, 5:6], rstat[:, 4:5]),
               ["rstat"], ["rstat"])
            op("vector",
               lambda e: e.tensor_scalar(x2f_sb[:], s1_sb[:], rstat[:, 1:2], rstat[:, 5:6],
                                         op0=AluOpType.subtract, op1=AluOpType.mult),
               ["s1_sb", "rstat"], ["x2f_sb"])
            op("vector", lambda e: e.tensor_copy(x2hi_sb[:], x2f_sb[:]), ["x2f_sb"], ["x2hi_sb"])
            op("vector",
               lambda e: e.tensor_tensor(x2lo_sb[:], x2f_sb[:], x2hi_sb[:],
                                         op=AluOpType.subtract),
               ["x2f_sb", "x2hi_sb"], ["x2lo_sb"])
            dma_("gpsimd", x2f[gt * 128:(gt + 1) * 128, :], x2f_sb[:], ["x2f_sb"], ["x2f_w"])
            for limb, (srct, srcn, dst_ap, dnm) in enumerate([
                    (x2hi_sb, "x2hi_sb", x2fm_hi, "x2fm_hi_w"),
                    (x2lo_sb, "x2lo_sb", x2fm_lo, "x2fm_lo_w")]):
                for dc in range(16):
                    half = dc % 2
                    op("tensor",
                       lambda e, srct=srct, dc=dc, half=half: e.transpose(
                           bkTT[half][:, 0:128],
                           srct[:, dc * 128:(dc + 1) * 128], idn16_sb[:]),
                       [srcn, "idn16_sb"], [f"bkT{half}"])
                    op("vector",
                       lambda e, limb=limb, dc=dc, half=half: e.tensor_copy(
                           x2stg[limb][:, dc * 128:(dc + 1) * 128],
                           bkTT[half][:, 0:128]),
                       [f"bkT{half}"], [f"x2stg{limb}"])
                dma_("gpsimd",
                     dst_ap[:, gt * 128:(gt + 1) * 128].rearrange("(c p) t -> p c t", p=128),
                     x2stg[limb][:, :].rearrange("p (c t) -> p c t", t=128),
                     [f"x2stg{limb}"], [dnm])

    def emit_P5a_tb(tb):
        t0 = tb * 4
        for t in range(4):
            gt = t0 + t
            for limb, (src_ap, nm) in enumerate([(x2fm_hi, "x2fm_hi_w"), (x2fm_lo, "x2fm_lo_w")]):
                dma_("sync",
                     ytb5[:, (t * 2 + limb) * 2048:(t * 2 + limb + 1) * 2048].rearrange(
                         "p (c t) -> p c t", t=128),
                     src_ap[:, gt * 128:(gt + 1) * 128].rearrange("(c p) t -> p c t", p=128),
                     [nm], ["ytb5"])
        for hp in range(16):
            for gi, col0 in enumerate([hp * 512, FFH + hp * 512]):
                for dc in range(16):
                    mi = dc % 3
                    dma_("sync", gmov[mi][:, :], guT[dc * 128:(dc + 1) * 128, col0:col0 + 512],
                         ["guT_w"], [f"gmov{mi}"])
                    for t in range(4):
                        for limb in range(2):
                            st = (dc == 0 and limb == 0)
                            sp = (dc == 15 and limb == 1)
                            y0 = (t * 2 + limb) * 2048 + dc * 128
                            op("tensor",
                               lambda e, y0=y0, mi=mi, t=t, st=st, sp=sp: e.matmul(
                                   bk[t][:, :], ytb5[:, y0:y0 + 128], gmov[mi][:, :],
                                   start=st, stop=sp),
                               ["ytb5", f"gmov{mi}"], [f"bk{t}"])
                for t in range(4):
                    if gi == 0:
                        op("scalar",
                           lambda e, t=t: e.activation(sg_sb[t][:], bk[t][:, :], AF.Silu),
                           [f"bk{t}"], [f"sgsb{t}"])
                    else:
                        gt = t0 + t
                        op("vector",
                           lambda e, t=t: e.tensor_tensor(hf_sb[:], sg_sb[t][:], bk[t][:, :],
                                                          op=AluOpType.mult),
                           [f"sgsb{t}", f"bk{t}"], ["hf_sb"])
                        op("vector", lambda e: e.tensor_copy(hhi_sb[:], hf_sb[:]),
                           ["hf_sb"], ["hhi_sb"])
                        op("vector",
                           lambda e: e.tensor_tensor(hlo_sb[:], hf_sb[:], hhi_sb[:],
                                                     op=AluOpType.subtract),
                           ["hf_sb", "hhi_sb"], ["hlo_sb"])
                        dma_("gpsimd", h_hi[gt * 128:(gt + 1) * 128, hp * 512:(hp + 1) * 512],
                             hhi_sb[:], ["hhi_sb"], ["h_hi_w"])
                        dma_("gpsimd", h_lo[gt * 128:(gt + 1) * 128, hp * 512:(hp + 1) * 512],
                             hlo_sb[:], ["hlo_sb"], ["h_lo_w"])

    def emit_P5b_prep(gt):
        for half in range(2):
            cols = slice(half * 4096, (half + 1) * 4096)
            dma_("sync", hl_in[0][:, :], h_hi[gt * 128:(gt + 1) * 128, cols], ["h_hi_w"],
                 ["hlin0"])
            dma_("sync", hl_in[1][:, :], h_lo[gt * 128:(gt + 1) * 128, cols], ["h_lo_w"],
                 ["hlin1"])
            op("vector",
               lambda e: e.tensor_tensor(hnf_sb[:], hl_in[0][:], hl_in[1][:],
                                         op=AluOpType.add),
               ["hlin0", "hlin1"], ["hnf_sb"])
            for c in range(8):
                op("scalar",
                   lambda e, c=c, half=half: e.activation(
                       hsqd[:], hnf_sb[:, c * 512:(c + 1) * 512], AF.Square,
                       accum_out=hsq_acc[:, half * 8 + c: half * 8 + c + 1]),
                   ["hnf_sb"], ["hsqd", "hsq_acc"])
        op("vector",
           lambda e: e.tensor_reduce(rstat[:, 0:1], hsq_acc[:, 0:16], axis=X,
                                     op=AluOpType.add),
           ["hsq_acc"], ["rstat"])
        op("scalar",
           lambda e: e.activation(rstat[:, 1:2], rstat[:, 0:1], AF.Sqrt,
                                  bias=eps_sb[:, 2:3], scale=1.0 / FFH),
           ["rstat", "eps_sb"], ["rstat"])
        op("vector", lambda e, gt=gt: e.reciprocal(rstd_ffn[:, gt:gt + 1], rstat[:, 1:2]),
           ["rstat"], ["rstd_ffn"])
        for half in range(2):
            cols = slice(half * 4096, (half + 1) * 4096)
            dma_("sync", hl_in[0][:, :], h_hi[gt * 128:(gt + 1) * 128, cols], ["h_hi_w"],
                 ["hlin0"])
            dma_("sync", hl_in[1][:, :], h_lo[gt * 128:(gt + 1) * 128, cols], ["h_lo_w"],
                 ["hlin1"])
            op("vector",
               lambda e: e.tensor_tensor(hnf_sb[:], hl_in[0][:], hl_in[1][:],
                                         op=AluOpType.add),
               ["hlin0", "hlin1"], ["hnf_sb"])
            op("vector",
               lambda e, gt=gt: e.tensor_scalar(hn16_sb[0][:], hnf_sb[:],
                                                rstd_ffn[:, gt:gt + 1], None,
                                                op0=AluOpType.mult),
               ["hnf_sb", "rstd_ffn"], ["hn16_0"])
            op("vector",
               lambda e, gt=gt: e.tensor_scalar(hnf_sb[:], hnf_sb[:],
                                                rstd_ffn[:, gt:gt + 1], None,
                                                op0=AluOpType.mult),
               ["hnf_sb", "rstd_ffn"], ["hnf_sb"])
            op("vector",
               lambda e: e.tensor_tensor(hn16_sb[1][:], hnf_sb[:], hn16_sb[0][:],
                                         op=AluOpType.subtract),
               ["hnf_sb", "hn16_0"], ["hn16_1"])
            for limb, (dst_ap, dnm) in enumerate([(hn_hi, "hn_hi_w"), (hn_lo, "hn_lo_w")]):
                for hc in range(32):
                    hhalf = hc % 2
                    op("tensor",
                       lambda e, limb=limb, hc=hc, hhalf=hhalf: e.transpose(
                           bkTT[hhalf][:, 0:128],
                           hn16_sb[limb][:, hc * 128:(hc + 1) * 128], idn16_sb[:]),
                       [f"hn16_{limb}", "idn16_sb"], [f"bkT{hhalf}"])
                    op("vector",
                       lambda e, limb=limb, hc=hc, hhalf=hhalf: e.tensor_copy(
                           hnstg[limb][:, hc * 128:(hc + 1) * 128],
                           bkTT[hhalf][:, 0:128]),
                       [f"bkT{hhalf}"], [f"hnstg{limb}"])
                dma_("gpsimd",
                     dst_ap[half * 4096:(half + 1) * 4096,
                            gt * 128:(gt + 1) * 128].rearrange("(c p) t -> p c t", p=128),
                     hnstg[limb][:, :].rearrange("p (c t) -> p c t", t=128),
                     [f"hnstg{limb}"], [dnm])

    def emit_P5b_proj(sb_i):
        tok0 = sb_i * 512
        for hq in range(4):
            for hc in range(16):
                for limb, (src_ap, nm) in enumerate([(hn_hi, "hn_hi_w"), (hn_lo, "hn_lo_w")]):
                    dma_("sync", hnq[hc * 2 + limb][:, :],
                         src_ap[hq * 2048 + hc * 128: hq * 2048 + (hc + 1) * 128,
                                tok0:tok0 + 512],
                         [nm], [f"hnq{hc * 2 + limb}"])
            for dcq in range(4):
                for hc in range(16):
                    pi = hc % 4
                    dma_("sync", pst[pi][:, :],
                         ppT[hq * 2048 + hc * 128: hq * 2048 + (hc + 1) * 128,
                             dcq * 512:(dcq + 1) * 512],
                         ["ppT_w"], [f"pst{pi}"])
                    for dc in range(4):
                        for limb in range(2):
                            st = (hc == 0 and limb == 0)
                            sp = (hc == 15 and limb == 1)
                            op("tensor",
                               lambda e, pi=pi, dc=dc, limb=limb, st=st, sp=sp, hc=hc: e.matmul(
                                   bk[dc][:, :], pst[pi][:, dc * 128:(dc + 1) * 128],
                                   hnq[hc * 2 + limb][:, :], start=st, stop=sp),
                               [f"pst{pi}", f"hnq{hc * 2 + limb}"], [f"bk{dc}"])
                for dc in range(4):
                    di = dcq * 4 + dc
                    if hq == 0:
                        op("vector",
                           lambda e, dc=dc, di=di: e.tensor_copy(o2acc[di][:], bk[dc][:, :]),
                           [f"bk{dc}"], [f"o2acc{di}"])
                    else:
                        op("vector",
                           lambda e, dc=dc, di=di: e.tensor_tensor(
                               o2acc[di][:], bk[dc][:, :], o2acc[di][:], op=AluOpType.add),
                           [f"bk{dc}", f"o2acc{di}"], [f"o2acc{di}"])
        for t in range(4):
            gt = sb_i * 4 + t
            for di in range(16):
                op("tensor",
                   lambda e, di=di, t=t: e.transpose(
                       bk[4][:, 0:128], o2acc[di][:, t * 128:(t + 1) * 128], idn32_sb[:]),
                   [f"o2acc{di}", "idn32_sb"], ["bk4"])
                op("vector",
                   lambda e, di=di: e.tensor_copy(outt[:, di * 128:(di + 1) * 128],
                                                  bk[4][:, 0:128]),
                   ["bk4"], ["outt"])
            dma_("sync", xres5[:], x2f[gt * 128:(gt + 1) * 128, :], ["x2f_w"], ["xres5"])
            op("vector",
               lambda e: e.scalar_tensor_tensor(outt[:], xres5[:], 1.0, outt[:],
                                                op0=AluOpType.mult, op1=AluOpType.add,
                                                accum_out=rstat[:, 0:1]),
               ["xres5", "outt"], ["outt", "rstat"])
            for c in range(4):
                op("scalar",
                   lambda e, c=c: e.activation(psq5[:], outt[:, c * 512:(c + 1) * 512],
                                               AF.Square, accum_out=rstat2[:, c:c + 1]),
                   ["outt"], ["psq5", "rstat2"])
            op("vector",
               lambda e: e.tensor_reduce(rstat2[:, 4:5], rstat2[:, 0:4], axis=X,
                                         op=AluOpType.add),
               ["rstat2"], ["rstat2"])
            op("vector",
               lambda e: e.tensor_scalar(rstat[:, 1:2], rstat[:, 0:1], 1.0 / 2048.0, None,
                                         op0=AluOpType.mult),
               ["rstat"], ["rstat"])
            op("vector",
               lambda e: e.tensor_tensor(rstat[:, 2:3], rstat[:, 1:2], rstat[:, 1:2],
                                         op=AluOpType.mult),
               ["rstat"], ["rstat"])
            op("vector",
               lambda e: e.scalar_tensor_tensor(rstat[:, 3:4], rstat2[:, 4:5], 1.0 / 2048.0,
                                                rstat[:, 2:3], op0=AluOpType.mult,
                                                op1=AluOpType.subtract),
               ["rstat2", "rstat"], ["rstat"])
            op("scalar",
               lambda e: e.activation(rstat[:, 4:5], rstat[:, 3:4], AF.Sqrt,
                                      bias=eps_sb[:, 1:2]),
               ["rstat", "eps_sb"], ["rstat"])
            op("vector", lambda e: e.reciprocal(rstat[:, 5:6], rstat[:, 4:5]),
               ["rstat"], ["rstat"])
            op("vector",
               lambda e: e.tensor_scalar(xres5[:], outt[:], rstat[:, 1:2], rstat[:, 5:6],
                                         op0=AluOpType.subtract, op1=AluOpType.mult),
               ["outt", "rstat"], ["xres5"])
            dma_("gpsimd", out[gt * 128:(gt + 1) * 128, :], xres5[:], ["xres5"], ["out_w"])

    if "P0" in phases:
        emit_P0()
    if "P1b" in phases:
        emit_P1b()
        P.barrier()
    if "P1" in phases and "P3" not in phases:
        for rb in range(2 * FFH // 128):
            emit_P1_gu(rb)
        for hb in range(FFH // 512):
            emit_P1_pp(hb)
        P.barrier()
    if "P2" in phases:
        for s in range(P2_KV_STRIPS):
            emit_P2_kv_strip(s)
        for qs in range(P2_Q_STRIPS):
            emit_P2_q_strip(qs)
        P.barrier()
    if "P3" in phases:
        quant_units = ([("gu", rb) for rb in range(2 * FFH // 128)] +
                       [("pp", hb) for hb in range(FFH // 512)]) if "P1" in phases else []
        per_head = (len(quant_units) + H - 1) // H
        for h in range(H):
            for kind, i in quant_units[h * per_head:(h + 1) * per_head]:
                (emit_P1_gu if kind == "gu" else emit_P1_pp)(i)
            emit_P3_head(h)
        P.barrier()
    if "P4" in phases:
        for tb in range(4):
            emit_P4_tb(tb)
        P.barrier()
    if "P5" in phases:
        for tb in range(4):
            emit_P5a_tb(tb)
        P.barrier()
        for gt in range(NSLOT):
            emit_P5b_prep(gt)
        for sb_i in range(4):
            emit_P5b_proj(sb_i)
    if TRUNC:
        P.ops = P.ops[:TRUNC]
    fence_reads = (["out_w"] if "P5" in phases else []) + sorted(debug_outputs_res(debug_outputs))
    P.add("gpsimd", "FENCE", fence_reads, [], dma=False, tag="fence")

    DMA_POOL = 24
    counts = P.resolve(dma_pool=DMA_POOL)

    sems = {}
    with stack:
        for e in ENGINES:
            sems[(e, -1)] = stack.enter_context(nc.semaphore(f"sem_{e}"))
        for q in ("sync", "gpsimd"):
            for s in range(DMA_POOL):
                sems[(q, s)] = stack.enter_context(nc.semaphore(f"dq_{q}_{s}"))
        fence_sem = stack.enter_context(nc.semaphore("fence_sem"))
        block = stack.enter_context(nc.Block())
        by_engine = {e: [] for e in ENGINES}
        for o in P.ops:
            by_engine[o.engine].append(o)

        def make_body(ename):
            def body(eng):
                for o in by_engine[ename]:
                    for (ws, wts) in o.waits:
                        eng.wait_ge(sems[ws], wts)
                    if o.is_barrier:
                        eng.sem_inc(sems[o.sem], 1)
                    elif o.fn == "FENCE":
                        eng.sem_inc(fence_sem, 1)
                    else:
                        inst = o.fn(eng)
                        if o.inc is not None:
                            inst.then_inc(sems[o.sem], o.inc)
            return body

        for ename in ENGINES:
            getattr(block, ename)(make_body(ename))

    return nc, counts, len(P.ops)


def debug_outputs_res(names):
    m = {"k_fm": "k_fm_w", "v_tok": "v_tok_w", "q_fm": "q_fm_w", "y_hi": "y_hi_w",
         "y_lo": "y_lo_w", "x2f": "x2f_w", "x2fm_hi": "x2fm_hi_w", "x2fm_lo": "x2fm_lo_w",
         "h_hi": "h_hi_w", "h_lo": "h_lo_w", "hn_hi": "hn_hi_w", "hn_lo": "hn_lo_w",
         "guT": "guT_w", "ppT": "ppT_w", "wq_hi": "wq_hi", "wq_lo": "wq_lo",
         "wu_hi": "wu_hi", "wu_lo": "wu_lo", "wd_hi": "wd_hi", "wd_lo": "wd_lo",
         "wp_hi": "wpA_hi", "wp_lo": "wpA_lo"}
    return [m[n] for n in names if n in m]


_CACHE = {}


def _get_program(debug_outputs=()):
    key = tuple(sorted(debug_outputs))
    if key not in _CACHE:
        _CACHE[key] = build(debug_outputs)
    return _CACHE[key]


def make_core_inputs(core, x, wq, wdn, wup, wpj, qgain, wgu, wpr):
    b, p = core // 2, core & 1
    cosk, sink = host_tables()
    xb = np.ascontiguousarray(x[b])
    x_own = np.ascontiguousarray(xb.reshape(32, 128, D)[p::2].reshape(NQ, D))
    cq = np.ascontiguousarray(cosk.reshape(32, 128, 32)[p::2].reshape(NQ, 32))
    sq = np.ascontiguousarray(sink.reshape(32, 128, 32)[p::2].reshape(NQ, 32))
    return {
        "x_kv": xb,
        "x_own": x_own,
        "w_q_krope": wq,
        "w_kv_down": wdn,
        "w_kv_up": wup,
        "w_proj": wpj,
        "q_gain": qgain.reshape(1, H).astype(np.float32),
        "w_gate_up": wgu,
        "w_ffn_proj": wpr,
        "cosk": cosk, "sink": sink, "cosq": cq, "sinq": sq,
        "maskt": host_mask(p),
        "idn16": np.eye(128, dtype=np.float32).astype(BF),
        "idn32": np.eye(128, dtype=np.float32),
        "ones1": np.ones((1, 128), np.float32),
        "epsil": np.tile(np.array([[1e-6, 1e-5, np.finfo(np.float32).eps, 0.0]],
                                  np.float32), (128, 1)),
    }


def kernel(L0_attn_x, L0_attn_w_q_krope, L0_attn_w_kv_down, L0_attn_w_kv_up,
           L0_attn_w_proj, L0_attn_q_gain, L0_ffn_w_gate_up, L0_ffn_w_proj):
    nc, counts, n_ops = _get_program()
    in_maps = [make_core_inputs(c, L0_attn_x, L0_attn_w_q_krope, L0_attn_w_kv_down,
                                L0_attn_w_kv_up, L0_attn_w_proj, L0_attn_q_gain,
                                L0_ffn_w_gate_up, L0_ffn_w_proj)
               for c in range(8)]
    res = run_bass_kernel_spmd(nc, in_maps, core_ids=list(range(8)))
    outf = np.empty((B_, S, D), np.float32)
    for c in range(8):
        b, p = c // 2, c & 1
        outf[b].reshape(32, 128, D)[p::2] = res.results[c]["out_rows"].reshape(NSLOT, 128, D)
    return outf


# version 04 (final)
# speedup vs baseline: 1.0941x; 1.0028x over previous
import contextlib
import numpy as np
import ml_dtypes
import concourse.bass as bass
import concourse.mybir as mybir
from concourse.bass_utils import run_bass_kernel_spmd
from concourse.alu_op_type import AluOpType

F32 = mybir.dt.float32
BF16 = mybir.dt.bfloat16
AF = mybir.ActivationFunctionType
BF = ml_dtypes.bfloat16
X = mybir.AxisListType.X

B_, S, D = 4, 4096, 2048
H, HD, ROPE, NOPE, LAT = 16, 128, 64, 64, 512
FFH = 8192
NQ = 2048
NSLOT = 16
INV_SQRT_HD = float(np.float32(1.0) / np.float32(np.sqrt(np.float32(128.0))))
SQRT_HD = float(np.sqrt(128.0))
TERN_B = float(0.5 + 2.0 ** -10)
F32EPS = float(np.finfo(np.float32).eps)

ENGINES = ("tensor", "vector", "scalar", "gpsimd", "sync")
P2_KV_STRIPS = 8
P2_Q_STRIPS = 4
P2_STAGE = "all"
TRUNC = 0


class Op:
    __slots__ = ("engine", "fn", "reads", "writes", "is_dma", "is_barrier",
                 "waits", "inc", "ts", "tag", "sem")

    def __init__(self, engine, fn, reads, writes, is_dma, tag=""):
        self.engine = engine
        self.fn = fn
        self.reads = tuple(reads)
        self.writes = tuple(writes)
        self.is_dma = is_dma
        self.is_barrier = False
        self.waits = []
        self.inc = None
        self.ts = 0
        self.sem = None
        self.tag = tag


class Prog:
    def __init__(self):
        self.ops = []

    def add(self, engine, fn, reads=(), writes=(), dma=False, tag=""):
        op = Op(engine, fn, reads, writes, dma, tag)
        self.ops.append(op)
        return op

    def barrier(self):
        op = Op("BARRIER", None, (), (), False, "barrier")
        op.is_barrier = True
        self.ops.append(op)

    def resolve(self, dma_pool=24):
        deps = {}
        last_writer = {}
        readers = {}
        barrier_frontier = []
        expanded = []

        def skip_edge(a, b, r):
            return (a.is_dma and b.is_dma and (r == "in" or r.endswith("_w")))

        for op in self.ops:
            if op.is_barrier:
                d = set(last_writer.values())
                for rl in readers.values():
                    d.update(rl)
                d.update(barrier_frontier)
                subs = []
                for e in ENGINES:
                    sub = Op(e, None, (), (), False, "barrier")
                    sub.is_barrier = True
                    deps[id(sub)] = set(d)
                    subs.append(sub)
                    expanded.append(sub)
                barrier_frontier = subs
                last_writer = {}
                readers = {}
                continue
            d = set(barrier_frontier)
            for r in op.reads:
                w = last_writer.get(r)
                if w is not None and not skip_edge(w, op, r):
                    d.add(w)
                readers.setdefault(r, []).append(op)
            for r in op.writes:
                w = last_writer.get(r)
                if w is not None and not skip_edge(w, op, r):
                    d.add(w)
                for rd in readers.get(r, ()):
                    if rd is not op and not skip_edge(rd, op, r):
                        d.add(rd)
                last_writer[r] = op
                readers[r] = []
            deps[id(op)] = d
            expanded.append(op)
        self.ops = expanded

        UNSAFE_SELF = {"vector", "scalar", "gpsimd"}
        needed = set()
        for op in self.ops:
            for dep in deps[id(op)]:
                if dep.engine != op.engine or dep.is_dma or dep.engine in UNSAFE_SELF:
                    needed.add(id(dep))

        eng_count = {}
        dma_idx = {}
        for op in self.ops:
            if op.is_dma:
                i = dma_idx.get(op.engine, 0)
                dma_idx[op.engine] = i + 1
                slot = i % dma_pool
                use = i // dma_pool + 1
                op.sem = (op.engine, slot)
                op.inc = 16
                op.ts = 16 * use
                if use > 1:
                    op.waits.append((op.sem, 16 * (use - 1)))
            elif id(op) in needed or op.is_barrier:
                c = eng_count.get(op.engine, 0) + 1
                eng_count[op.engine] = c
                op.sem = (op.engine, -1)
                op.inc = 1
                op.ts = c
            else:
                op.sem = None
                op.inc = None
                op.ts = 0
        frontier = {}
        for op in self.ops:
            fr = frontier.setdefault(op.engine, {})
            want = {}
            for dep in deps[id(op)]:
                if (dep.engine == op.engine and not dep.is_dma
                        and dep.engine not in UNSAFE_SELF):
                    continue
                if dep.sem is None:
                    continue
                if dep.ts > want.get(dep.sem, 0):
                    want[dep.sem] = dep.ts
            for sem, ts in sorted(want.items()):
                if fr.get(sem, 0) >= ts:
                    continue
                op.waits.append((sem, ts))
                fr[sem] = ts
        self.dma_idx = dma_idx
        return eng_count


def host_tables():
    inv = (np.float32(10000.0) ** (-(np.arange(0, ROPE, 2).astype(np.float32)) / np.float32(ROPE))).astype(np.float32)
    t = np.arange(S, dtype=np.float32)
    fr = (t[:, None] * inv[None, :]).astype(np.float32)
    return np.cos(fr).astype(np.float32), np.sin(fr).astype(np.float32)


def host_mask(p):
    r = np.arange(128)[:, None]
    c = np.arange(256)[None, :]
    tail = np.where(c <= p * 128 + r, np.float32(0.0), np.float32(-1e30))
    return np.concatenate([np.zeros((128, 256), np.float32), tail], 1).astype(BF)


def strips_for_slot(j):
    blocks = 2 * j + 2
    tail = 2 if blocks % 4 == 2 else 4
    full = (blocks - tail) // 4
    return [(i * 512, 512) for i in range(full)] + [(full * 512, tail * 128)]


class Arena:

    def __init__(self, handle, n_elems):
        self.h = handle
        self.n = n_elems
        self.off = 0

    def base(self, off):
        self.off = off
        return self

    def take(self, shape, dt):
        n = shape[1] * (2 if dt == F32 else 1)
        off = (self.off + 63) & ~63
        self.off = off + n
        assert self.off <= self.n, f"arena overflow at {self.off} > {self.n}"
        sl = self.h[0:shape[0], off:off + n]
        if dt != BF16:
            sl = sl.bitcast(dt)
        return sl


def build(debug_outputs=(), phases=('P0','P1b','P1','P2','P3','P4','P5')):
    nc = bass.Bass("TRN2", target_bir_lowering=False)
    P = Prog()
    stack = contextlib.ExitStack()
    debug_outputs = set(debug_outputs)

    def inp(name, shape, dt):
        return nc.dram_tensor(name, list(shape), dt, kind="ExternalInput").ap()

    def outp(name, shape, dt):
        return nc.dram_tensor(name, list(shape), dt, kind="ExternalOutput").ap()

    def scratch(name, shape, dt):
        if name in debug_outputs:
            return nc.dram_tensor(name, list(shape), dt, kind="ExternalOutput").ap()
        return nc.dram_tensor(name, list(shape), dt).ap()

    def sbuf(name, shape, dt):
        return stack.enter_context(nc.sbuf_tensor(name, list(shape), dt))

    dma_ = lambda q, o, i, r, w, tag="": P.add(
        q, (lambda e, o=o, i=i: e.dma_start(out=o, in_=i)), r, w, dma=True, tag=tag)
    op = lambda eng, fn, r, w, tag="": P.add(eng, fn, r, w, dma=False, tag=tag)

    x_kv = inp("x_kv", (S, D), F32)
    x_own = inp("x_own", (NQ, D), F32)
    wq = inp("w_q_krope", (D, 3072), F32)
    wd = inp("w_kv_down", (D, LAT), F32)
    wu = inp("w_kv_up", (LAT, 3072), F32)
    wp = inp("w_proj", (D, D), F32)
    qg = inp("q_gain", (1, H), F32)
    wgu = inp("w_gate_up", (2 * FFH, D), F32)
    wpr = inp("w_ffn_proj", (D, FFH), F32)
    cosk = inp("cosk", (S, 32), F32)
    sink = inp("sink", (S, 32), F32)
    cosq = inp("cosq", (NQ, 32), F32)
    sinq = inp("sinq", (NQ, 32), F32)
    maskt = inp("maskt", (128, 512), BF16)
    idn16 = inp("idn16", (128, 128), BF16)
    idn32 = inp("idn32", (128, 128), F32)
    ones1 = inp("ones1", (1, 128), F32)
    epsil = inp("epsil", (128, 4), F32)
    out = outp("out_rows", (NQ, D), F32)

    wq_hi = scratch("wq_hi", (D, 3072), BF16)
    wq_lo = scratch("wq_lo", (D, 3072), BF16)
    wd_hi = scratch("wd_hi", (D, LAT), BF16)
    wd_lo = scratch("wd_lo", (D, LAT), BF16)
    wu_hi = scratch("wu_hi", (LAT, 3072), BF16)
    wu_lo = scratch("wu_lo", (LAT, 3072), BF16)
    wp_hi = scratch("wp_hi", (D, D), BF16)
    wp_lo = scratch("wp_lo", (D, D), BF16)
    guT = scratch("guT", (D, 2 * FFH), BF16)
    ppT = scratch("ppT", (FFH, D), BF16)
    q_fm = scratch("q_fm", (H, HD, NQ), F32)
    k_fm = scratch("k_fm", (H, HD, S), F32)
    v_tok = scratch("v_tok", (H, S, HD), F32)
    y_hi = scratch("y_hi", (H, HD, NQ), BF16)
    y_lo = scratch("y_lo", (H, HD, NQ), BF16)
    x2f = scratch("x2f", (NQ, D), F32)
    x2fm_hi = scratch("x2fm_hi", (D, NQ), BF16)
    x2fm_lo = scratch("x2fm_lo", (D, NQ), BF16)
    h_hi = scratch("h_hi", (NQ, FFH), BF16)
    h_lo = scratch("h_lo", (NQ, FFH), BF16)
    hn_hi = scratch("hn_hi", (FFH, NQ), BF16)
    hn_lo = scratch("hn_lo", (FFH, NQ), BF16)

    idn16_sb = sbuf("idn16_sb", (128, 128), BF16)
    idn32_sb = sbuf("idn32_sb", (128, 128), F32)
    mask_sb = sbuf("mask_sb", (128, 512), BF16)
    ones_sb = sbuf("ones_sb", (1, 128), F32)
    qg_sb = sbuf("qg_sb", (1, H), F32)
    qg_tmp = sbuf("qg_tmp", (1, H), F32)
    gain_b = sbuf("gain_b", (128, H), F32)
    negc_b = sbuf("negc_b", (128, H), F32)
    rstat = sbuf("rstat", (128, 16), F32)
    rstat2 = sbuf("rstat2", (128, 16), F32)
    rstd_ffn = sbuf("rstd_ffn", (128, NSLOT), F32)
    hsq_acc = sbuf("hsq_acc", (128, 16), F32)
    rs_sb = sbuf("rs_sb", (128, 16), F32)
    rsum = sbuf("rsum", (128, 2), F32)
    rrow = sbuf("rrow", (1, 128), F32)
    R_sb = sbuf("R_sb", (128, 128), F32)
    yf_sb = sbuf("yf_sb", (128, 128), F32)
    yhi_sb = sbuf("yhi_sb", (128, 128), BF16)
    ylo_sb = sbuf("ylo_sb", (128, 128), BF16)
    eps_sb = sbuf("eps_sb", (128, 4), F32)
    qsc = sbuf("qsc", (128, 32), BF16)
    qrs = sbuf("qrs", (128, 32), F32)
    costl = sbuf("costl", (128, 32), F32)
    sintl = sbuf("sintl", (128, 32), F32)
    ropet = [sbuf(f"ropet{i}", (128, 32), F32) for i in range(4)]

    AR_N = 88064
    AR = sbuf("AR", (128, AR_N), BF16)
    ar = Arena(AR[:], AR_N)

    ar.base(0)
    wf = [ar.take((128, 3072), F32) for _ in range(2)]
    w16p = [ar.take((128, 3072), BF16) for _ in range(2)]
    wlop = [ar.take((128, 3072), BF16) for _ in range(2)]

    ar.base(0)
    qw16 = [ar.take((128, 2048), BF16) for _ in range(2)]
    qm32 = ar.take((128, 2048), F32)
    qaux = ar.take((128, 2048), BF16)
    qq_t = ar.take((128, 2048), BF16)
    stg_gu = [ar.take((128, 512), BF16) for _ in range(16)]
    stg_pp = [ar.take((128, 2048), BF16) for _ in range(4)]
    QUANT_END = ar.off

    ar.base(0)
    xtm = [ar.take((128, D), F32) for _ in range(2)]
    xh_t = [ar.take((128, D), BF16) for _ in range(4)]
    xl_t = [ar.take((128, D), BF16) for _ in range(4)]
    xfm_hi = [ar.take((128, 512), BF16) for _ in range(16)]
    xfm_lo = [ar.take((128, 512), BF16) for _ in range(16)]
    wmov = [ar.take((128, 512), BF16) for _ in range(4)]
    praw = [ar.take((128, 512), F32) for _ in range(2)]
    psq = ar.take((128, 512), F32)
    pnrm = ar.take((128, 512), F32)
    ckv_hi = [ar.take((128, 512), BF16) for _ in range(4)]
    ckv_lo = [ar.take((128, 512), BF16) for _ in range(4)]
    ckv_f = ar.take((128, 512), F32)
    ckv_h2 = ar.take((128, 512), BF16)
    ckv_l2 = ar.take((128, 512), BF16)
    kfm_sb = [ar.take((128, 512), F32) for _ in range(H)]
    P2_END = ar.off

    ar.base(30720)
    q_sb = ar.take((128, NQ), F32)
    k_sb = [ar.take((128, 512), F32) for _ in range(2)]
    v_sb = [ar.take((128, 512), F32) for _ in range(2)]
    probs_sb = [ar.take((128, 512), F32) for _ in range(2)]
    pT_sb = [ar.take((128, 512), F32) for _ in range(2)]
    P3_END = ar.off

    ar.base(0)
    ytb = ar.take((128, 16384), BF16)
    o1_sb = [ar.take((128, D), F32) for _ in range(4)]
    xres = [ar.take((128, D), F32) for _ in range(2)]
    s1_sb = ar.take((128, D), F32)
    x2f_sb = ar.take((128, D), F32)
    x2hi_sb = ar.take((128, D), BF16)
    x2lo_sb = ar.take((128, D), BF16)
    x2stg = [ar.take((128, D), BF16) for _ in range(2)]
    psq4 = ar.take((128, 512), F32)
    wmv4 = [ar.take((128, 512), BF16) for _ in range(4)]
    P4_END = ar.off

    ar.base(0)
    ytb5 = ar.take((128, 16384), BF16)
    gmov = [ar.take((128, 512), BF16) for _ in range(3)]
    sg_sb = [ar.take((128, 512), F32) for _ in range(4)]
    hf_sb = ar.take((128, 512), F32)
    hhi_sb = ar.take((128, 512), BF16)
    hlo_sb = ar.take((128, 512), BF16)
    hsqd = ar.take((128, 512), F32)
    P5A_END = ar.off

    ar.base(0)
    hl_in = [ar.take((128, 4096), BF16) for _ in range(2)]
    hnf_sb = ar.take((128, 4096), F32)
    hn16_sb = [ar.take((128, 4096), BF16) for _ in range(2)]
    hnstg = [ar.take((128, 4096), BF16) for _ in range(2)]
    P5BP_END = ar.off

    ar.base(36864)
    hnq = [ar.take((128, 512), BF16) for _ in range(32)]
    pst = [ar.take((128, 512), BF16) for _ in range(4)]
    o2acc = [ar.take((128, 512), F32) for _ in range(16)]
    outt = ar.take((128, D), F32)
    xres5 = ar.take((128, D), F32)
    psq5 = ar.take((128, 512), F32)
    P5BJ_END = ar.off
    assert max(P2_END, P3_END, P4_END, P5A_END, P5BP_END, P5BJ_END) <= AR_N

    bk = [stack.enter_context(nc.psum_tensor(f"bk{i}", [128, 512], F32)) for i in range(6)]
    bkTT = [stack.enter_context(nc.psum_tensor(f"bkTT{i}", [128, 1024], BF16)) for i in range(2)]

    def bkT_ap(half, c0, c1):
        return bkTT[half][:, c0:c0 + (c1 - c0)]

    OUT_RES = ["out_w"]

    def emit_P0():
        dma_("sync", idn16_sb[:], idn16[:, :], ["in"], ["idn16_sb"])
        dma_("sync", idn32_sb[:], idn32[:, :], ["in"], ["idn32_sb"])
        dma_("sync", mask_sb[:], maskt[:, :], ["in"], ["mask_sb"])
        dma_("sync", ones_sb[:], ones1[:, :], ["in"], ["ones_sb"])
        dma_("sync", qg_sb[:], qg[:, :], ["in"], ["qg_sb"])
        dma_("sync", eps_sb[:], epsil[:, :], ["in"], ["eps_sb"])
        op("tensor", lambda e: e.matmul(bk[0][:, 0:H], ones_sb[:], qg_sb[:], start=True, stop=True),
           ["ones_sb", "qg_sb"], ["bk0"])
        op("vector", lambda e: e.tensor_copy(gain_b[:], bk[0][:, 0:H]), ["bk0"], ["gain_b"])
        op("scalar", lambda e: e.activation(qg_tmp[:], qg_sb[:], AF.Abs), ["qg_sb"], ["qg_tmp"])
        op("scalar", lambda e: e.mul(qg_tmp[:], qg_tmp[:], -SQRT_HD), ["qg_tmp"], ["qg_tmp"])
        op("tensor", lambda e: e.matmul(bk[0][:, 0:H], ones_sb[:], qg_tmp[:], start=True, stop=True),
           ["ones_sb", "qg_tmp"], ["bk0"])
        op("vector", lambda e: e.tensor_copy(negc_b[:], bk[0][:, 0:H]), ["bk0"], ["negc_b"])

    def emit_P1b():
        jobs = [(wq, wq_hi, wq_lo, D, 3072, "wq"), (wd, wd_hi, wd_lo, D, LAT, "wd"),
                (wu, wu_hi, wu_lo, LAT, 3072, "wu"), (wp, wp_hi, wp_lo, D, D, "wpA")]
        for (w_in, o_hi, o_lo, RR, CC, nm) in jobs:
            for r in range(RR // 128):
                i = r % 2
                rows = slice(r * 128, (r + 1) * 128)
                dma_("sync", wf[i][:, 0:CC], w_in[rows, :], ["in"], [f"wf{i}"])
                dma_("gpsimd", w16p[i][:, 0:CC], w_in[rows, :], ["in"], [f"w16p{i}"])
                op("vector",
                   lambda e, i=i, CC=CC: e.tensor_tensor(wlop[i][:, 0:CC], wf[i][:, 0:CC],
                                                         w16p[i][:, 0:CC], op=AluOpType.subtract),
                   [f"wf{i}", f"w16p{i}"], [f"wlop{i}"])
                dma_("gpsimd", o_hi[rows, :], w16p[i][:, 0:CC], [f"w16p{i}"], [nm + "_hi"])
                dma_("gpsimd", o_lo[rows, :], wlop[i][:, 0:CC], [f"wlop{i}"], [nm + "_lo"])

    def _quant_tile(i, CC):
        G = CC // 64
        op("vector",
           lambda e, i=i, CC=CC, G=G: e.tensor_reduce(
               qrs[:, 0:G], qw16[i][:, 0:CC].rearrange("p (g k) -> p g k", k=64),
               axis=X, op=AluOpType.add, apply_absolute_value=True),
           [f"qw16{i}"], ["qrs"])
        op("vector",
           lambda e, G=G: e.tensor_scalar(qsc[:, 0:G], qrs[:, 0:G], 1.0 / 64.0, 1e-8,
                                          op0=AluOpType.mult, op1=AluOpType.max),
           ["qrs"], ["qsc"])
        op("vector", lambda e, G=G: e.reciprocal(qrs[:, 0:G], qsc[:, 0:G]), ["qsc"], ["qrs"])
        op("vector",
           lambda e, i=i, CC=CC, G=G: e.tensor_tensor(
               qm32[:, 0:CC].rearrange("p (g k) -> p g k", k=64),
               qw16[i][:, 0:CC].rearrange("p (g k) -> p g k", k=64),
               qrs[:, 0:G].unsqueeze(2).broadcast_to([128, G, 64]),
               op=AluOpType.mult),
           [f"qw16{i}", "qrs"], ["qm32"])
        op("vector",
           lambda e, CC=CC: e.tensor_scalar(qaux[:, 0:CC], qm32[:, 0:CC], -TERN_B, None,
                                            op0=AluOpType.is_lt),
           ["qm32"], ["qaux"])
        op("vector",
           lambda e, CC=CC: e.scalar_tensor_tensor(qq_t[:, 0:CC], qm32[:, 0:CC], TERN_B,
                                                   qaux[:, 0:CC], op0=AluOpType.is_gt,
                                                   op1=AluOpType.subtract),
           ["qm32", "qaux"], ["qq_t"])
        op("vector",
           lambda e, CC=CC, G=G: e.tensor_tensor(
               qq_t[:, 0:CC].rearrange("p (g k) -> p g k", k=64),
               qq_t[:, 0:CC].rearrange("p (g k) -> p g k", k=64),
               qsc[:, 0:G].unsqueeze(2).broadcast_to([128, G, 64]),
               op=AluOpType.mult),
           ["qq_t", "qsc"], ["qq_t"])
        op("vector",
           lambda e, i=i, CC=CC: e.tensor_tensor(qaux[:, 0:CC], qq_t[:, 0:CC],
                                                 qw16[i][:, 0:CC], op=AluOpType.subtract),
           ["qq_t", f"qw16{i}"], ["qaux"])
        op("vector",
           lambda e, i=i, CC=CC: e.tensor_tensor(qw16[i][:, 0:CC], qw16[i][:, 0:CC],
                                                 qaux[:, 0:CC], op=AluOpType.add),
           [f"qw16{i}", "qaux"], [f"qw16{i}"])

    def emit_P1_gu(rb):
        i = rb % 2
        rows = slice(rb * 128, (rb + 1) * 128)
        dma_("gpsimd", qw16[i][:, 0:D], wgu[rows, :], ["in"], [f"qw16{i}"])
        _quant_tile(i, D)
        for c in range(16):
            half = (c % 8) // 4
            sc_ = (c % 4) * 128
            op("tensor",
               lambda e, i=i, c=c, half=half, sc_=sc_: e.transpose(
                   bkTT[half][:, sc_:sc_ + 128],
                   qw16[i][:, c * 128:(c + 1) * 128], idn16_sb[:]),
               [f"qw16{i}", "idn16_sb"], [f"bkT{half}"])
            op("vector",
               lambda e, c=c, rb=rb, half=half, sc_=sc_: e.tensor_copy(
                   stg_gu[c][:, (rb % 4) * 128:(rb % 4 + 1) * 128],
                   bkTT[half][:, sc_:sc_ + 128]),
               [f"bkT{half}"], [f"stg_gu{c}"])
        if rb % 4 == 3:
            for c in range(16):
                dma_("gpsimd", guT[c * 128:(c + 1) * 128, (rb - 3) * 128:(rb + 1) * 128],
                     stg_gu[c][:, :], [f"stg_gu{c}"], ["guT_w"])

    def emit_P1_pp(hb):
        for db in range(16):
            i = db % 2
            rows = slice(db * 128, (db + 1) * 128)
            dma_("gpsimd", qw16[i][:, 0:512], wpr[rows, hb * 512:(hb + 1) * 512],
                 ["in"], [f"qw16{i}"])
            _quant_tile(i, 512)
            for c in range(4):
                half = db % 2
                sc_ = c * 128
                op("tensor",
                   lambda e, i=i, c=c, half=half, sc_=sc_: e.transpose(
                       bkTT[half][:, sc_:sc_ + 128],
                       qw16[i][:, c * 128:(c + 1) * 128], idn16_sb[:]),
                   [f"qw16{i}", "idn16_sb"], [f"bkT{half}"])
                op("vector",
                   lambda e, c=c, db=db, half=half, sc_=sc_: e.tensor_copy(
                       stg_pp[c][:, db * 128:(db + 1) * 128],
                       bkTT[half][:, sc_:sc_ + 128]),
                   [f"bkT{half}"], [f"stg_pp{c}"])
        for c in range(4):
            dma_("gpsimd", ppT[hb * 512 + c * 128: hb * 512 + (c + 1) * 128, :],
                 stg_pp[c][:, :], [f"stg_pp{c}"], ["ppT_w"])

    def load_xfm(src_ap, row0):
        for t in range(4):
            i = t % 2
            rows = slice(row0 + t * 128, row0 + (t + 1) * 128)
            dma_("sync", xtm[i][:], src_ap[rows, :], ["in"], [f"xtm{i}"])
            op("vector", lambda e, i=i, t=t: e.tensor_copy(xh_t[t][:], xtm[i][:]),
               [f"xtm{i}"], [f"xh{t}"])
            op("vector",
               lambda e, i=i, t=t: e.tensor_tensor(xl_t[t][:], xtm[i][:], xh_t[t][:],
                                                   op=AluOpType.subtract),
               [f"xtm{i}", f"xh{t}"], [f"xl{t}"])
        for limb in range(2):
            src_t = xh_t if limb == 0 else xl_t
            pre = "xh" if limb == 0 else "xl"
            dst = xfm_hi if limb == 0 else xfm_lo
            dnm = "xfmh" if limb == 0 else "xfml"
            for dc in range(16):
                half = dc % 2
                for t in range(4):
                    op("tensor",
                       lambda e, src_t=src_t, t=t, dc=dc, half=half: e.transpose(
                           bkTT[half][:, t * 128:(t + 1) * 128],
                           src_t[t][:, dc * 128:(dc + 1) * 128], idn16_sb[:]),
                       [pre + str(t), "idn16_sb"], [f"bkT{half}"])
                op("vector",
                   lambda e, dst=dst, dc=dc, half=half: e.tensor_copy(
                       dst[dc][:, :], bkTT[half][:, 0:512]),
                   [f"bkT{half}"], [f"{dnm}{dc}"])

    def proj_group(whi_ap, wlo_ap, wc0, hires, lores):
        for dc in range(16):
            wi = (dc % 2) * 2
            dma_("sync", wmov[wi][:, :], whi_ap[dc * 128:(dc + 1) * 128, wc0:wc0 + 512],
                 [hires], [f"wmov{wi}"])
            dma_("sync", wmov[wi + 1][:, :], wlo_ap[dc * 128:(dc + 1) * 128, wc0:wc0 + 512],
                 [lores], [f"wmov{wi + 1}"])
            for t in range(4):
                plan = [(xfm_hi, "xfmh", wi), (xfm_lo, "xfml", wi), (xfm_hi, "xfmh", wi + 1)]
                for pi, (xt, xn, wj) in enumerate(plan):
                    st = (dc == 0 and pi == 0)
                    sp = (dc == 15 and pi == 2)
                    op("tensor",
                       lambda e, xt=xt, dc=dc, t=t, wj=wj, st=st, sp=sp: e.matmul(
                           bk[t][:, :], xt[dc][:, t * 128:(t + 1) * 128],
                           wmov[wj][:, :], start=st, stop=sp),
                       [f"{xn}{dc}", f"wmov{wj}"], [f"bk{t}"])

    def kvup_group(g):
        for l in range(4):
            wi = (l % 2) * 2
            dma_("sync", wmov[wi][:, :], wu_hi[l * 128:(l + 1) * 128, g * 512:(g + 1) * 512],
                 ["wu_hi"], [f"wmov{wi}"])
            dma_("sync", wmov[wi + 1][:, :], wu_lo[l * 128:(l + 1) * 128, g * 512:(g + 1) * 512],
                 ["wu_lo"], [f"wmov{wi + 1}"])
            for t in range(4):
                plan = [(ckv_hi, "ckvh", wi), (ckv_lo, "ckvl", wi), (ckv_hi, "ckvh", wi + 1)]
                for pi, (ct, cn, wj) in enumerate(plan):
                    st = (l == 0 and pi == 0)
                    sp = (l == 3 and pi == 2)
                    op("tensor",
                       lambda e, ct=ct, l=l, t=t, wj=wj, st=st, sp=sp: e.matmul(
                           bk[t][:, :], ct[l][:, t * 128:(t + 1) * 128],
                           wmov[wj][:, :], start=st, stop=sp),
                       [f"{cn}{l}", f"wmov{wj}"], [f"bk{t}"])

    def rms_units(t, bank_name, bank_ap, with_rope, cos_ap, sin_ap, row0,
                  head0, gain, units_per_head, dst_rows_fixed=None):
        pr, prn = praw[t % 2], f"praw{t % 2}"
        op("scalar", lambda e: e.copy(pr[:], bank_ap[:, :]), [bank_name], [prn])
        op("scalar", lambda e: e.activation(psq[:], pr[:], AF.Square), [prn], ["psq"])
        op("vector",
           lambda e: e.tensor_reduce(rstat[:, 0:8], psq[:].rearrange("p (g k) -> p g k", k=64),
                                     axis=X, op=AluOpType.add),
           ["psq"], ["rstat"])
        op("scalar",
           lambda e: e.activation(rstat2[:, 0:8], rstat[:, 0:8], AF.Sqrt,
                                  bias=eps_sb[:, 0:1], scale=1.0 / 64.0),
           ["rstat", "eps_sb"], ["rstat2"])
        op("vector", lambda e: e.reciprocal(rstat[:, 0:8], rstat2[:, 0:8]), ["rstat2"], ["rstat"])
        if gain:
            for c in range(4):
                hh = head0 + c
                op("vector",
                   lambda e, c=c, hh=hh: e.tensor_scalar(rstat[:, 2 * c:2 * c + 2],
                                                         rstat[:, 2 * c:2 * c + 2],
                                                         gain_b[:, hh:hh + 1], None,
                                                         op0=AluOpType.mult),
                   ["rstat", "gain_b"], ["rstat"])
        if with_rope:
            dma_("sync", costl[:], cos_ap[row0:row0 + 128, :], ["in"], ["costl"])
            dma_("sync", sintl[:], sin_ap[row0:row0 + 128, :], ["in"], ["sintl"])
        for u in range(8):
            u0 = u * 64
            rope_u = with_rope and (units_per_head == 1 or (u % 2 == 1))
            if rope_u:
                for (di, a_sl, ct, cn) in [(0, slice(u0, u0 + 32), costl, "costl"),
                                           (1, slice(u0 + 32, u0 + 64), sintl, "sintl"),
                                           (2, slice(u0, u0 + 32), sintl, "sintl"),
                                           (3, slice(u0 + 32, u0 + 64), costl, "costl")]:
                    op("vector",
                       lambda e, di=di, a_sl=a_sl, ct=ct, pr=pr: e.tensor_tensor(
                           ropet[di][:], pr[:, a_sl], ct[:], op=AluOpType.mult),
                       [prn, cn], [f"ropet{di}"])
                op("vector",
                   lambda e, u0=u0: e.tensor_tensor(pnrm[:, u0:u0 + 32], ropet[0][:],
                                                    ropet[1][:], op=AluOpType.add),
                   ["ropet0", "ropet1"], ["pnrm"])
                op("vector",
                   lambda e, u0=u0: e.tensor_tensor(pnrm[:, u0 + 32:u0 + 64], ropet[3][:],
                                                    ropet[2][:], op=AluOpType.subtract),
                   ["ropet3", "ropet2"], ["pnrm"])
                src, srcn = pnrm, "pnrm"
            else:
                src, srcn = pr, prn
            op("vector",
               lambda e, u=u, u0=u0, src=src: e.tensor_scalar(
                   pnrm[:, u0:u0 + 64], src[:, u0:u0 + 64], rstat[:, u:u + 1], None,
                   op0=AluOpType.mult),
               [srcn, "rstat"], ["pnrm"])
        for up in range(4):
            u0 = up * 128
            op("tensor",
               lambda e, u0=u0: e.transpose(bk[4][:, 0:128], pnrm[:, u0:u0 + 128],
                                            idn32_sb[:]),
               ["pnrm", "idn32_sb"], ["bk4"])
            if units_per_head == 2:
                head = head0 + up
                op("vector",
                   lambda e, head=head, t=t: e.tensor_copy(
                       kfm_sb[head][:, t * 128:(t + 1) * 128], bk[4][:, 0:128]),
                   ["bk4"], [f"kfmsb{head}"])
            else:
                for half in range(2):
                    head = head0 + 2 * up + half
                    op("vector",
                       lambda e, head=head, half=half, t=t: e.tensor_copy(
                           kfm_sb[head][dst_rows_fixed:dst_rows_fixed + 64,
                                        t * 128:(t + 1) * 128],
                           bk[4][half * 64:(half + 1) * 64, 0:128]),
                       ["bk4"], [f"kfmsb{head}"])

    def emit_P2_kv_strip(s):
        T0 = s * 512
        load_xfm(x_kv, T0)
        if P2_STAGE == "xfm":
            return
        for g, (whi_ap, wlo_ap, wc0, hr, lr) in enumerate([
                (wq_hi, wq_lo, 2048, "wq_hi", "wq_lo"),
                (wq_hi, wq_lo, 2560, "wq_hi", "wq_lo"),
                (wd_hi, wd_lo, 0, "wd_hi", "wd_lo")]):
            proj_group(whi_ap, wlo_ap, wc0, hr, lr)
            if P2_STAGE == "proj":
                continue
            for t in range(4):
                if g < 2:
                    rms_units(t, f"bk{t}", bk[t], True, cosk, sink, T0 + t * 128,
                              g * 8, False, 1, dst_rows_fixed=64)
                else:
                    pr, prn = praw[t % 2], f"praw{t % 2}"
                    op("scalar", lambda e, t=t, pr=pr: e.copy(pr[:], bk[t][:, :]),
                       [f"bk{t}"], [prn])
                    op("scalar", lambda e, pr=pr: e.activation(psq[:], pr[:], AF.Square),
                       [prn], ["psq"])
                    op("vector",
                       lambda e: e.tensor_reduce(rstat[:, 0:1], psq[:], axis=X,
                                                 op=AluOpType.add),
                       ["psq"], ["rstat"])
                    op("scalar",
                       lambda e: e.activation(rstat2[:, 0:1], rstat[:, 0:1], AF.Sqrt,
                                              bias=eps_sb[:, 0:1], scale=1.0 / 512.0),
                       ["rstat", "eps_sb"], ["rstat2"])
                    op("vector", lambda e: e.reciprocal(rstat[:, 0:1], rstat2[:, 0:1]),
                       ["rstat2"], ["rstat"])
                    op("vector",
                       lambda e, pr=pr: e.tensor_scalar(ckv_f[:], pr[:], rstat[:, 0:1],
                                                        None, op0=AluOpType.mult),
                       [prn, "rstat"], ["ckvf"])
                    op("vector", lambda e: e.tensor_copy(ckv_h2[:], ckv_f[:]),
                       ["ckvf"], ["ckvh2"])
                    op("vector",
                       lambda e: e.tensor_tensor(ckv_l2[:], ckv_f[:], ckv_h2[:],
                                                 op=AluOpType.subtract),
                       ["ckvf", "ckvh2"], ["ckvl2"])
                    for limb in range(2):
                        srct = ckv_h2 if limb == 0 else ckv_l2
                        srcn = "ckvh2" if limb == 0 else "ckvl2"
                        dstt = ckv_hi if limb == 0 else ckv_lo
                        dnm = "ckvh" if limb == 0 else "ckvl"
                        for l in range(4):
                            half = l % 2
                            op("tensor",
                               lambda e, srct=srct, l=l, half=half: e.transpose(
                                   bkTT[half][:, 0:128],
                                   srct[:, l * 128:(l + 1) * 128], idn16_sb[:]),
                               [srcn, "idn16_sb"], [f"bkT{half}"])
                            op("vector",
                               lambda e, dstt=dstt, l=l, t=t, half=half: e.tensor_copy(
                                   dstt[l][:, t * 128:(t + 1) * 128],
                                   bkTT[half][:, 0:128]),
                               [f"bkT{half}"], [f"{dnm}{l}"])
        if P2_STAGE in ("xfm", "proj", "post"):
            for head in range(H):
                dma_("gpsimd", k_fm[head, :, T0:T0 + 512], kfm_sb[head][:, :],
                     [f"kfmsb{head}"], ["k_fm_w"])
            return
        for g in range(6):
            kvup_group(g)
            for t in range(4):
                if g < 2:
                    rms_units(t, f"bk{t}", bk[t], False, None, None, 0,
                              g * 8, False, 1, dst_rows_fixed=0)
                else:
                    pr, prn = praw[t % 2], f"praw{t % 2}"
                    op("scalar", lambda e, t=t, pr=pr: e.copy(pr[:], bk[t][:, :]),
                       [f"bk{t}"], [prn])
                    for c in range(4):
                        head = (g - 2) * 4 + c
                        dma_("gpsimd", v_tok[head, T0 + t * 128: T0 + (t + 1) * 128, :],
                             pr[:, c * 128:(c + 1) * 128], [prn], ["v_tok_w"])
        for head in range(H):
            dma_("gpsimd", k_fm[head, :, T0:T0 + 512], kfm_sb[head][:, :],
                 [f"kfmsb{head}"], ["k_fm_w"])

    def emit_P2_q_strip(qs):
        T0 = qs * 512
        load_xfm(x_own, T0)
        for g in range(4):
            proj_group(wq_hi, wq_lo, g * 512, "wq_hi", "wq_lo")
            for t in range(4):
                rms_units(t, f"bk{t}", bk[t], True, cosq, sinq, T0 + t * 128,
                          g * 4, True, 2)
        for head in range(H):
            dma_("gpsimd", q_fm[head, :, T0:T0 + 512], kfm_sb[head][:, :],
                 [f"kfmsb{head}"], ["q_fm_w"])

    def emit_P3_head(h):
        dma_("sync", q_sb[:], q_fm[h, :, :], ["q_fm_w"], ["q_sb"])
        for j in range(NSLOT):
            strips = strips_for_slot(j)
            nst = len(strips)

            def qk_exp(si):
                k0, kw = strips[si]
                bi = si % 2
                sc = 2 + bi
                last = (si == nst - 1)
                dma_("sync", k_sb[bi][:, 0:kw], k_fm[h, :, k0:k0 + kw], ["k_fm_w"],
                     [f"k_sb{bi}"])
                dma_("sync", v_sb[bi][:, 0:kw].rearrange("p (c t) -> p c t", t=128),
                     v_tok[h, k0:k0 + kw, :].rearrange("(c p) t -> p c t", p=128),
                     ["v_tok_w"], [f"v_sb{bi}"])
                if last:
                    op("tensor",
                       lambda e, sc=sc, kw=kw: e.matmul(bk[sc][:, 0:kw], idn16_sb[:],
                                                        mask_sb[:, 512 - kw:512],
                                                        start=True, stop=False),
                       ["idn16_sb", "mask_sb"], [f"bk{sc}"])
                op("tensor",
                   lambda e, sc=sc, j=j, kw=kw, bi=bi, st=(not last): e.matmul(
                       bk[sc][:, 0:kw], q_sb[:, j * 128:(j + 1) * 128],
                       k_sb[bi][:, 0:kw], start=st, stop=True),
                   ["q_sb", f"k_sb{bi}"], [f"bk{sc}"])
                op("scalar",
                   lambda e, sc=sc, kw=kw, bi=bi, h=h, si=si: e.activation(
                       probs_sb[bi][:, 0:kw], bk[sc][:, 0:kw], AF.Exp,
                       bias=negc_b[:, h:h + 1], scale=INV_SQRT_HD,
                       accum_out=rs_sb[:, si:si + 1]),
                   [f"bk{sc}", "negc_b"], [f"probs{bi}", "rs_sb"])

            def t_pv(si):
                k0, kw = strips[si]
                bi = si % 2
                last = (si == nst - 1)
                for c in range(kw // 128):
                    op("tensor",
                       lambda e, bi=bi, c=c: e.transpose(
                           bk[4][:, c * 128:(c + 1) * 128],
                           probs_sb[bi][:, c * 128:(c + 1) * 128], idn32_sb[:]),
                       [f"probs{bi}", "idn32_sb"], ["bk4"])
                eng = "vector" if si % 2 == 0 else "scalar"
                if eng == "vector":
                    op("vector",
                       lambda e, bi=bi, kw=kw: e.tensor_copy(pT_sb[bi][:, 0:kw],
                                                             bk[4][:, 0:kw]),
                       ["bk4"], [f"pT{bi}"])
                else:
                    op("scalar",
                       lambda e, bi=bi, kw=kw: e.copy(pT_sb[bi][:, 0:kw], bk[4][:, 0:kw]),
                       ["bk4"], [f"pT{bi}"])
                for c in range(kw // 128):
                    st = (si == 0 and c == 0)
                    sp = (last and c == kw // 128 - 1)
                    op("tensor",
                       lambda e, bi=bi, c=c, st=st, sp=sp: e.matmul(
                           bk[0][:, 0:128], v_sb[bi][:, c * 128:(c + 1) * 128],
                           pT_sb[bi][:, c * 128:(c + 1) * 128], start=st, stop=sp),
                       [f"v_sb{bi}", f"pT{bi}"], ["bk0"])

            for si in range(nst):
                qk_exp(si)
                if si >= 1:
                    t_pv(si - 1)
            t_pv(nst - 1)
            op("vector",
               lambda e, nst=nst: e.tensor_reduce(rsum[:, 0:1], rs_sb[:, 0:nst], axis=X,
                                                  op=AluOpType.add),
               ["rs_sb"], ["rsum"])
            op("vector", lambda e: e.reciprocal(rsum[:, 1:2], rsum[:, 0:1]), ["rsum"], ["rsum"])
            op("tensor",
               lambda e: e.transpose(bk[5][0:1, 0:128], rsum[:, 1:2], idn32_sb[:]),
               ["rsum", "idn32_sb"], ["bk5"])
            op("vector", lambda e: e.tensor_copy(rrow[:, :], bk[5][0:1, 0:128]),
               ["bk5"], ["rrow"])
            op("tensor",
               lambda e: e.matmul(bk[5][:, 0:128], ones_sb[:], rrow[:, :],
                                  start=True, stop=True),
               ["ones_sb", "rrow"], ["bk5"])
            op("vector", lambda e: e.tensor_copy(R_sb[:], bk[5][:, 0:128]), ["bk5"], ["R_sb"])
            op("vector",
               lambda e: e.tensor_tensor(yf_sb[:], bk[0][:, 0:128], R_sb[:],
                                         op=AluOpType.mult),
               ["bk0", "R_sb"], ["yf_sb"])
            op("vector", lambda e: e.tensor_copy(yhi_sb[:], yf_sb[:]), ["yf_sb"], ["yhi_sb"])
            op("vector",
               lambda e: e.tensor_tensor(ylo_sb[:], yf_sb[:], yhi_sb[:],
                                         op=AluOpType.subtract),
               ["yf_sb", "yhi_sb"], ["ylo_sb"])
            dma_("gpsimd", y_hi[h, :, j * 128:(j + 1) * 128], yhi_sb[:], ["yhi_sb"], ["y_hi_w"])
            dma_("gpsimd", y_lo[h, :, j * 128:(j + 1) * 128], ylo_sb[:], ["ylo_sb"], ["y_lo_w"])

    def emit_P4_tb(tb):
        t0 = tb * 4
        for fc in range(H):
            for limb, (src, nm) in enumerate([(y_hi, "y_hi_w"), (y_lo, "y_lo_w")]):
                dma_("sync", ytb[:, fc * 1024 + limb * 512: fc * 1024 + (limb + 1) * 512],
                     src[fc, :, t0 * 128:(t0 + 4) * 128], [nm], ["ytb"])
        for dgrp in range(4):
            for fc in range(H):
                wi = (fc % 2) * 2
                dma_("sync", wmv4[wi][:, :],
                     wp_hi[fc * 128:(fc + 1) * 128, dgrp * 512:(dgrp + 1) * 512],
                     ["wpA_hi"], [f"wmv4{wi}"])
                dma_("sync", wmv4[wi + 1][:, :],
                     wp_lo[fc * 128:(fc + 1) * 128, dgrp * 512:(dgrp + 1) * 512],
                     ["wpA_lo"], [f"wmv4{wi + 1}"])
                for t in range(4):
                    yh0 = fc * 1024 + t * 128
                    yl0 = fc * 1024 + 512 + t * 128
                    for pi, (y0, wj) in enumerate([(yh0, wi), (yl0, wi), (yh0, wi + 1)]):
                        st = (fc == 0 and pi == 0)
                        sp = (fc == 15 and pi == 2)
                        op("tensor",
                           lambda e, y0=y0, wj=wj, t=t, st=st, sp=sp: e.matmul(
                               bk[t][:, :], ytb[:, y0:y0 + 128], wmv4[wj][:, :],
                               start=st, stop=sp),
                           ["ytb", f"wmv4{wj}"], [f"bk{t}"])
            for t in range(4):
                op("scalar",
                   lambda e, t=t, dgrp=dgrp: e.copy(o1_sb[t][:, dgrp * 512:(dgrp + 1) * 512],
                                                    bk[t][:, :]),
                   [f"bk{t}"], [f"o1sb{t}"])
        for t in range(4):
            gt = t0 + t
            xi = t % 2
            dma_("sync", xres[xi][:], x_own[gt * 128:(gt + 1) * 128, :], ["in"], [f"xres{xi}"])
            op("vector",
               lambda e, xi=xi, t=t: e.scalar_tensor_tensor(
                   s1_sb[:], xres[xi][:], 1.0, o1_sb[t][:], op0=AluOpType.mult,
                   op1=AluOpType.add, accum_out=rstat[:, 0:1]),
               [f"xres{xi}", f"o1sb{t}"], ["s1_sb", "rstat"])
            for c in range(4):
                op("scalar",
                   lambda e, c=c: e.activation(psq4[:], s1_sb[:, c * 512:(c + 1) * 512],
                                               AF.Square, accum_out=rstat2[:, c:c + 1]),
                   ["s1_sb"], ["psq4", "rstat2"])
            op("vector",
               lambda e: e.tensor_reduce(rstat2[:, 4:5], rstat2[:, 0:4], axis=X,
                                         op=AluOpType.add),
               ["rstat2"], ["rstat2"])
            op("vector",
               lambda e: e.tensor_scalar(rstat[:, 1:2], rstat[:, 0:1], 1.0 / 2048.0, None,
                                         op0=AluOpType.mult),
               ["rstat"], ["rstat"])
            op("vector",
               lambda e: e.tensor_tensor(rstat[:, 2:3], rstat[:, 1:2], rstat[:, 1:2],
                                         op=AluOpType.mult),
               ["rstat"], ["rstat"])
            op("vector",
               lambda e: e.scalar_tensor_tensor(rstat[:, 3:4], rstat2[:, 4:5], 1.0 / 2048.0,
                                                rstat[:, 2:3], op0=AluOpType.mult,
                                                op1=AluOpType.subtract),
               ["rstat2", "rstat"], ["rstat"])
            op("scalar",
               lambda e: e.activation(rstat[:, 4:5], rstat[:, 3:4], AF.Sqrt,
                                      bias=eps_sb[:, 1:2]),
               ["rstat", "eps_sb"], ["rstat"])
            op("vector", lambda e: e.reciprocal(rstat[:, 5:6], rstat[:, 4:5]),
               ["rstat"], ["rstat"])
            op("vector",
               lambda e: e.tensor_scalar(x2f_sb[:], s1_sb[:], rstat[:, 1:2], rstat[:, 5:6],
                                         op0=AluOpType.subtract, op1=AluOpType.mult),
               ["s1_sb", "rstat"], ["x2f_sb"])
            op("vector", lambda e: e.tensor_copy(x2hi_sb[:], x2f_sb[:]), ["x2f_sb"], ["x2hi_sb"])
            op("vector",
               lambda e: e.tensor_tensor(x2lo_sb[:], x2f_sb[:], x2hi_sb[:],
                                         op=AluOpType.subtract),
               ["x2f_sb", "x2hi_sb"], ["x2lo_sb"])
            dma_("gpsimd", x2f[gt * 128:(gt + 1) * 128, :], x2f_sb[:], ["x2f_sb"], ["x2f_w"])
            for limb, (srct, srcn, dst_ap, dnm) in enumerate([
                    (x2hi_sb, "x2hi_sb", x2fm_hi, "x2fm_hi_w"),
                    (x2lo_sb, "x2lo_sb", x2fm_lo, "x2fm_lo_w")]):
                for dc in range(16):
                    half = dc % 2
                    op("tensor",
                       lambda e, srct=srct, dc=dc, half=half: e.transpose(
                           bkTT[half][:, 0:128],
                           srct[:, dc * 128:(dc + 1) * 128], idn16_sb[:]),
                       [srcn, "idn16_sb"], [f"bkT{half}"])
                    op("vector",
                       lambda e, limb=limb, dc=dc, half=half: e.tensor_copy(
                           x2stg[limb][:, dc * 128:(dc + 1) * 128],
                           bkTT[half][:, 0:128]),
                       [f"bkT{half}"], [f"x2stg{limb}"])
                dma_("gpsimd",
                     dst_ap[:, gt * 128:(gt + 1) * 128].rearrange("(c p) t -> p c t", p=128),
                     x2stg[limb][:, :].rearrange("p (c t) -> p c t", t=128),
                     [f"x2stg{limb}"], [dnm])

    def emit_P5a_tb(tb):
        t0 = tb * 4
        for t in range(4):
            gt = t0 + t
            for limb, (src_ap, nm) in enumerate([(x2fm_hi, "x2fm_hi_w"), (x2fm_lo, "x2fm_lo_w")]):
                dma_("sync",
                     ytb5[:, (t * 2 + limb) * 2048:(t * 2 + limb + 1) * 2048].rearrange(
                         "p (c t) -> p c t", t=128),
                     src_ap[:, gt * 128:(gt + 1) * 128].rearrange("(c p) t -> p c t", p=128),
                     [nm], ["ytb5"])
        for hp in range(16):
            for gi, col0 in enumerate([hp * 512, FFH + hp * 512]):
                for dc in range(16):
                    mi = dc % 3
                    dma_("sync", gmov[mi][:, :], guT[dc * 128:(dc + 1) * 128, col0:col0 + 512],
                         ["guT_w"], [f"gmov{mi}"])
                    for t in range(4):
                        for limb in range(2):
                            st = (dc == 0 and limb == 0)
                            sp = (dc == 15 and limb == 1)
                            y0 = (t * 2 + limb) * 2048 + dc * 128
                            op("tensor",
                               lambda e, y0=y0, mi=mi, t=t, st=st, sp=sp: e.matmul(
                                   bk[t][:, :], ytb5[:, y0:y0 + 128], gmov[mi][:, :],
                                   start=st, stop=sp),
                               ["ytb5", f"gmov{mi}"], [f"bk{t}"])
                for t in range(4):
                    if gi == 0:
                        op("scalar",
                           lambda e, t=t: e.activation(sg_sb[t][:], bk[t][:, :], AF.Silu),
                           [f"bk{t}"], [f"sgsb{t}"])
                    else:
                        gt = t0 + t
                        op("vector",
                           lambda e, t=t: e.tensor_tensor(hf_sb[:], sg_sb[t][:], bk[t][:, :],
                                                          op=AluOpType.mult),
                           [f"sgsb{t}", f"bk{t}"], ["hf_sb"])
                        op("vector", lambda e: e.tensor_copy(hhi_sb[:], hf_sb[:]),
                           ["hf_sb"], ["hhi_sb"])
                        op("vector",
                           lambda e: e.tensor_tensor(hlo_sb[:], hf_sb[:], hhi_sb[:],
                                                     op=AluOpType.subtract),
                           ["hf_sb", "hhi_sb"], ["hlo_sb"])
                        dma_("gpsimd", h_hi[gt * 128:(gt + 1) * 128, hp * 512:(hp + 1) * 512],
                             hhi_sb[:], ["hhi_sb"], ["h_hi_w"])
                        dma_("gpsimd", h_lo[gt * 128:(gt + 1) * 128, hp * 512:(hp + 1) * 512],
                             hlo_sb[:], ["hlo_sb"], ["h_lo_w"])

    def emit_P5b_prep(gt):
        for half in range(2):
            cols = slice(half * 4096, (half + 1) * 4096)
            dma_("sync", hl_in[0][:, :], h_hi[gt * 128:(gt + 1) * 128, cols], ["h_hi_w"],
                 ["hlin0"])
            dma_("sync", hl_in[1][:, :], h_lo[gt * 128:(gt + 1) * 128, cols], ["h_lo_w"],
                 ["hlin1"])
            op("vector",
               lambda e: e.tensor_tensor(hnf_sb[:], hl_in[0][:], hl_in[1][:],
                                         op=AluOpType.add),
               ["hlin0", "hlin1"], ["hnf_sb"])
            for c in range(8):
                op("scalar",
                   lambda e, c=c, half=half: e.activation(
                       hsqd[:], hnf_sb[:, c * 512:(c + 1) * 512], AF.Square,
                       accum_out=hsq_acc[:, half * 8 + c: half * 8 + c + 1]),
                   ["hnf_sb"], ["hsqd", "hsq_acc"])
        op("vector",
           lambda e: e.tensor_reduce(rstat[:, 0:1], hsq_acc[:, 0:16], axis=X,
                                     op=AluOpType.add),
           ["hsq_acc"], ["rstat"])
        op("scalar",
           lambda e: e.activation(rstat[:, 1:2], rstat[:, 0:1], AF.Sqrt,
                                  bias=eps_sb[:, 2:3], scale=1.0 / FFH),
           ["rstat", "eps_sb"], ["rstat"])
        op("vector", lambda e, gt=gt: e.reciprocal(rstd_ffn[:, gt:gt + 1], rstat[:, 1:2]),
           ["rstat"], ["rstd_ffn"])
        for half in range(2):
            cols = slice(half * 4096, (half + 1) * 4096)
            dma_("sync", hl_in[0][:, :], h_hi[gt * 128:(gt + 1) * 128, cols], ["h_hi_w"],
                 ["hlin0"])
            dma_("sync", hl_in[1][:, :], h_lo[gt * 128:(gt + 1) * 128, cols], ["h_lo_w"],
                 ["hlin1"])
            op("vector",
               lambda e: e.tensor_tensor(hnf_sb[:], hl_in[0][:], hl_in[1][:],
                                         op=AluOpType.add),
               ["hlin0", "hlin1"], ["hnf_sb"])
            op("vector",
               lambda e, gt=gt: e.tensor_scalar(hn16_sb[0][:], hnf_sb[:],
                                                rstd_ffn[:, gt:gt + 1], None,
                                                op0=AluOpType.mult),
               ["hnf_sb", "rstd_ffn"], ["hn16_0"])
            op("vector",
               lambda e, gt=gt: e.tensor_scalar(hnf_sb[:], hnf_sb[:],
                                                rstd_ffn[:, gt:gt + 1], None,
                                                op0=AluOpType.mult),
               ["hnf_sb", "rstd_ffn"], ["hnf_sb"])
            op("vector",
               lambda e: e.tensor_tensor(hn16_sb[1][:], hnf_sb[:], hn16_sb[0][:],
                                         op=AluOpType.subtract),
               ["hnf_sb", "hn16_0"], ["hn16_1"])
            for limb, (dst_ap, dnm) in enumerate([(hn_hi, "hn_hi_w"), (hn_lo, "hn_lo_w")]):
                for hc in range(32):
                    hhalf = hc % 2
                    op("tensor",
                       lambda e, limb=limb, hc=hc, hhalf=hhalf: e.transpose(
                           bkTT[hhalf][:, 0:128],
                           hn16_sb[limb][:, hc * 128:(hc + 1) * 128], idn16_sb[:]),
                       [f"hn16_{limb}", "idn16_sb"], [f"bkT{hhalf}"])
                    op("vector",
                       lambda e, limb=limb, hc=hc, hhalf=hhalf: e.tensor_copy(
                           hnstg[limb][:, hc * 128:(hc + 1) * 128],
                           bkTT[hhalf][:, 0:128]),
                       [f"bkT{hhalf}"], [f"hnstg{limb}"])
                dma_("gpsimd",
                     dst_ap[half * 4096:(half + 1) * 4096,
                            gt * 128:(gt + 1) * 128].rearrange("(c p) t -> p c t", p=128),
                     hnstg[limb][:, :].rearrange("p (c t) -> p c t", t=128),
                     [f"hnstg{limb}"], [dnm])

    def emit_P5b_proj(sb_i):
        tok0 = sb_i * 512
        for hq in range(4):
            for hc in range(16):
                for limb, (src_ap, nm) in enumerate([(hn_hi, "hn_hi_w"), (hn_lo, "hn_lo_w")]):
                    dma_("sync", hnq[hc * 2 + limb][:, :],
                         src_ap[hq * 2048 + hc * 128: hq * 2048 + (hc + 1) * 128,
                                tok0:tok0 + 512],
                         [nm], [f"hnq{hc * 2 + limb}"])
            for dcq in range(4):
                for hc in range(16):
                    pi = hc % 4
                    dma_("sync", pst[pi][:, :],
                         ppT[hq * 2048 + hc * 128: hq * 2048 + (hc + 1) * 128,
                             dcq * 512:(dcq + 1) * 512],
                         ["ppT_w"], [f"pst{pi}"])
                    for dc in range(4):
                        for limb in range(2):
                            st = (hc == 0 and limb == 0)
                            sp = (hc == 15 and limb == 1)
                            op("tensor",
                               lambda e, pi=pi, dc=dc, limb=limb, st=st, sp=sp, hc=hc: e.matmul(
                                   bk[dc][:, :], pst[pi][:, dc * 128:(dc + 1) * 128],
                                   hnq[hc * 2 + limb][:, :], start=st, stop=sp),
                               [f"pst{pi}", f"hnq{hc * 2 + limb}"], [f"bk{dc}"])
                for dc in range(4):
                    di = dcq * 4 + dc
                    if hq == 0:
                        op("vector",
                           lambda e, dc=dc, di=di: e.tensor_copy(o2acc[di][:], bk[dc][:, :]),
                           [f"bk{dc}"], [f"o2acc{di}"])
                    else:
                        op("vector",
                           lambda e, dc=dc, di=di: e.tensor_tensor(
                               o2acc[di][:], bk[dc][:, :], o2acc[di][:], op=AluOpType.add),
                           [f"bk{dc}", f"o2acc{di}"], [f"o2acc{di}"])
        for t in range(4):
            gt = sb_i * 4 + t
            for di in range(16):
                op("tensor",
                   lambda e, di=di, t=t: e.transpose(
                       bk[4][:, 0:128], o2acc[di][:, t * 128:(t + 1) * 128], idn32_sb[:]),
                   [f"o2acc{di}", "idn32_sb"], ["bk4"])
                op("vector",
                   lambda e, di=di: e.tensor_copy(outt[:, di * 128:(di + 1) * 128],
                                                  bk[4][:, 0:128]),
                   ["bk4"], ["outt"])
            dma_("sync", xres5[:], x2f[gt * 128:(gt + 1) * 128, :], ["x2f_w"], ["xres5"])
            op("vector",
               lambda e: e.scalar_tensor_tensor(outt[:], xres5[:], 1.0, outt[:],
                                                op0=AluOpType.mult, op1=AluOpType.add,
                                                accum_out=rstat[:, 0:1]),
               ["xres5", "outt"], ["outt", "rstat"])
            for c in range(4):
                op("scalar",
                   lambda e, c=c: e.activation(psq5[:], outt[:, c * 512:(c + 1) * 512],
                                               AF.Square, accum_out=rstat2[:, c:c + 1]),
                   ["outt"], ["psq5", "rstat2"])
            op("vector",
               lambda e: e.tensor_reduce(rstat2[:, 4:5], rstat2[:, 0:4], axis=X,
                                         op=AluOpType.add),
               ["rstat2"], ["rstat2"])
            op("vector",
               lambda e: e.tensor_scalar(rstat[:, 1:2], rstat[:, 0:1], 1.0 / 2048.0, None,
                                         op0=AluOpType.mult),
               ["rstat"], ["rstat"])
            op("vector",
               lambda e: e.tensor_tensor(rstat[:, 2:3], rstat[:, 1:2], rstat[:, 1:2],
                                         op=AluOpType.mult),
               ["rstat"], ["rstat"])
            op("vector",
               lambda e: e.scalar_tensor_tensor(rstat[:, 3:4], rstat2[:, 4:5], 1.0 / 2048.0,
                                                rstat[:, 2:3], op0=AluOpType.mult,
                                                op1=AluOpType.subtract),
               ["rstat2", "rstat"], ["rstat"])
            op("scalar",
               lambda e: e.activation(rstat[:, 4:5], rstat[:, 3:4], AF.Sqrt,
                                      bias=eps_sb[:, 1:2]),
               ["rstat", "eps_sb"], ["rstat"])
            op("vector", lambda e: e.reciprocal(rstat[:, 5:6], rstat[:, 4:5]),
               ["rstat"], ["rstat"])
            op("vector",
               lambda e: e.tensor_scalar(xres5[:], outt[:], rstat[:, 1:2], rstat[:, 5:6],
                                         op0=AluOpType.subtract, op1=AluOpType.mult),
               ["outt", "rstat"], ["xres5"])
            dma_("gpsimd", out[gt * 128:(gt + 1) * 128, :], xres5[:], ["xres5"], ["out_w"])

    if "P0" in phases:
        emit_P0()
    if "P1b" in phases:
        emit_P1b()
        P.barrier()
    if "P1" in phases and "P3" not in phases:
        for rb in range(2 * FFH // 128):
            emit_P1_gu(rb)
        for hb in range(FFH // 512):
            emit_P1_pp(hb)
        P.barrier()
    if "P2" in phases:
        for s in range(P2_KV_STRIPS):
            emit_P2_kv_strip(s)
        for qs in range(P2_Q_STRIPS):
            emit_P2_q_strip(qs)
        P.barrier()
    if "P3" in phases:
        quant_units = ([("gu", rb) for rb in range(2 * FFH // 128)] +
                       [("pp", hb) for hb in range(FFH // 512)]) if "P1" in phases else []
        per_head = (len(quant_units) + H - 1) // H
        for h in range(H):
            for kind, i in quant_units[h * per_head:(h + 1) * per_head]:
                (emit_P1_gu if kind == "gu" else emit_P1_pp)(i)
            emit_P3_head(h)
        P.barrier()
    if "P4" in phases:
        for tb in range(4):
            emit_P4_tb(tb)
        P.barrier()
    if "P5" in phases:
        for tb in range(4):
            emit_P5a_tb(tb)
        P.barrier()
        for gt in range(NSLOT):
            emit_P5b_prep(gt)
        for sb_i in range(4):
            emit_P5b_proj(sb_i)
    if TRUNC:
        P.ops = P.ops[:TRUNC]
    fence_reads = (["out_w"] if "P5" in phases else []) + sorted(debug_outputs_res(debug_outputs))
    P.add("gpsimd", "FENCE", fence_reads, [], dma=False, tag="fence")

    DMA_POOL = 24
    counts = P.resolve(dma_pool=DMA_POOL)

    sems = {}
    with stack:
        for e in ENGINES:
            sems[(e, -1)] = stack.enter_context(nc.semaphore(f"sem_{e}"))
        for q in ("sync", "gpsimd"):
            for s in range(DMA_POOL):
                sems[(q, s)] = stack.enter_context(nc.semaphore(f"dq_{q}_{s}"))
        fence_sem = stack.enter_context(nc.semaphore("fence_sem"))
        block = stack.enter_context(nc.Block())
        by_engine = {e: [] for e in ENGINES}
        for o in P.ops:
            by_engine[o.engine].append(o)

        def make_body(ename):
            def body(eng):
                for o in by_engine[ename]:
                    for (ws, wts) in o.waits:
                        eng.wait_ge(sems[ws], wts)
                    if o.is_barrier:
                        eng.sem_inc(sems[o.sem], 1)
                    elif o.fn == "FENCE":
                        eng.sem_inc(fence_sem, 1)
                    else:
                        inst = o.fn(eng)
                        if o.inc is not None:
                            inst.then_inc(sems[o.sem], o.inc)
            return body

        for ename in ENGINES:
            getattr(block, ename)(make_body(ename))

    return nc, counts, len(P.ops)


def debug_outputs_res(names):
    m = {"k_fm": "k_fm_w", "v_tok": "v_tok_w", "q_fm": "q_fm_w", "y_hi": "y_hi_w",
         "y_lo": "y_lo_w", "x2f": "x2f_w", "x2fm_hi": "x2fm_hi_w", "x2fm_lo": "x2fm_lo_w",
         "h_hi": "h_hi_w", "h_lo": "h_lo_w", "hn_hi": "hn_hi_w", "hn_lo": "hn_lo_w",
         "guT": "guT_w", "ppT": "ppT_w", "wq_hi": "wq_hi", "wq_lo": "wq_lo",
         "wu_hi": "wu_hi", "wu_lo": "wu_lo", "wd_hi": "wd_hi", "wd_lo": "wd_lo",
         "wp_hi": "wpA_hi", "wp_lo": "wpA_lo"}
    return [m[n] for n in names if n in m]


_CACHE = {}


def _get_program(debug_outputs=()):
    key = tuple(sorted(debug_outputs))
    if key not in _CACHE:
        _CACHE[key] = build(debug_outputs)
    return _CACHE[key]


def make_core_inputs(core, x, wq, wdn, wup, wpj, qgain, wgu, wpr):
    b, p = core // 2, core & 1
    cosk, sink = host_tables()
    xb = np.ascontiguousarray(x[b])
    x_own = np.ascontiguousarray(xb.reshape(32, 128, D)[p::2].reshape(NQ, D))
    cq = np.ascontiguousarray(cosk.reshape(32, 128, 32)[p::2].reshape(NQ, 32))
    sq = np.ascontiguousarray(sink.reshape(32, 128, 32)[p::2].reshape(NQ, 32))
    return {
        "x_kv": xb,
        "x_own": x_own,
        "w_q_krope": wq,
        "w_kv_down": wdn,
        "w_kv_up": wup,
        "w_proj": wpj,
        "q_gain": qgain.reshape(1, H).astype(np.float32),
        "w_gate_up": wgu,
        "w_ffn_proj": wpr,
        "cosk": cosk, "sink": sink, "cosq": cq, "sinq": sq,
        "maskt": host_mask(p),
        "idn16": np.eye(128, dtype=np.float32).astype(BF),
        "idn32": np.eye(128, dtype=np.float32),
        "ones1": np.ones((1, 128), np.float32),
        "epsil": np.tile(np.array([[1e-6, 1e-5, np.finfo(np.float32).eps, 0.0]],
                                  np.float32), (128, 1)),
    }


def kernel(L0_attn_x, L0_attn_w_q_krope, L0_attn_w_kv_down, L0_attn_w_kv_up,
           L0_attn_w_proj, L0_attn_q_gain, L0_ffn_w_gate_up, L0_ffn_w_proj):
    nc, counts, n_ops = _get_program()
    in_maps = [make_core_inputs(c, L0_attn_x, L0_attn_w_q_krope, L0_attn_w_kv_down,
                                L0_attn_w_kv_up, L0_attn_w_proj, L0_attn_q_gain,
                                L0_ffn_w_gate_up, L0_ffn_w_proj)
               for c in range(8)]
    res = run_bass_kernel_spmd(nc, in_maps, core_ids=list(range(8)))
    outf = np.empty((B_, S, D), np.float32)
    for c in range(8):
        b, p = c // 2, c & 1
        outf[b].reshape(32, 128, D)[p::2] = res.results[c]["out_rows"].reshape(NSLOT, 128, D)
    return outf
